# version 17
# baseline (speedup 1.0000x reference)
import numpy as np
import ml_dtypes
import concourse.bass as bass
import concourse.mybir as mybir
from concourse.bass_utils import run_bass_kernel_spmd

F32 = mybir.dt.float32
BF16 = mybir.dt.bfloat16
AF = mybir.ActivationFunctionType
ALU = mybir.AluOpType
AX = mybir.AxisListType
BF = ml_dtypes.bfloat16

NCORES = 8
D = 1024
T = 2048
HT = 1024
NDT = 8
DEPTH = 6
NH = 16
DH = 64
WIN = 256
DI = 2730
PI = float(np.pi)
QK_SCALE = 8.0
NEG = -1.0e30

FFN_CHUNKS = [(k * 128, 128) for k in range(21)] + [(2688, 42)]
NCH = len(FFN_CHUNKS)

DEBUG_TAPS = []
LAST = None


def host_constants():
    C = {}
    C["identb"] = np.eye(128, dtype=np.float32).astype(BF)
    C["identf"] = np.eye(128, dtype=np.float32)
    P = np.zeros((128, 128), np.float32)
    for b in (0, 64):
        for d in range(64):
            P[b + (d + 32) % 64, b + d] = 1.0
    C["pp"] = P.astype(BF)
    i = np.arange(128)[:, None]
    jj = np.arange(512)[None, :]
    valid = jj < 384
    m_main = np.where((jj >= i) & (jj <= i + 256) & valid, 0.0, NEG)
    m_w0q0 = np.where((jj >= 256) & (jj <= 256 + i) & valid, 0.0, NEG)
    m_w0q1 = np.where((jj >= 128) & (jj <= 256 + i) & valid, 0.0, NEG)
    C["masks"] = np.concatenate([m_main, m_w0q0, m_w0q1], 1).astype(BF)
    C["onesc"] = np.ones((128, 1), np.float32).astype(BF)
    C["onesf"] = np.ones((1, 128), np.float32)
    C["epscol"] = np.full((128, 1), 1e-5, np.float32)
    C["negpicol"] = np.full((128, 1), -PI, np.float32)
    hs = np.zeros((128, 8 * 16), np.float32)
    hb = np.zeros((16, 8 * 128), np.float32)
    for m in range(8):
        for p in range(128):
            c = 2 * m + p // 64
            hs[p, m * 16 + c] = 1.0
            hb[c, m * 128 + p] = 1.0
    C["hsum"] = hs.astype(BF)
    C["hbm"] = hb.astype(BF)
    half = np.arange(0, DH, 2, dtype=np.float64)
    inv_freq = 1.0 / 10000.0 ** (half / DH)
    t = np.arange(T, dtype=np.float64)
    fr = t[:, None] * inv_freq[None, :]
    freqs = np.concatenate([fr, fr], -1)
    base_scale = (half + 0.4 * DH) / (1.4 * DH)
    power = (t - T // 2) / (WIN // 2)
    scale = base_scale[None, :] ** power[:, None]
    scale = np.concatenate([scale, scale], -1)
    sign = np.where(np.arange(DH) < 32, 1.0, -1.0)[None, :]
    csk = np.cos(freqs) / scale
    ssk = np.sin(freqs) / scale * sign
    scl2 = QK_SCALE * scale * scale
    for nm, tab in (("csk", csk), ("ssk", ssk), ("scl2", scl2)):
        C[nm] = np.tile(tab.T, (2, 1)).astype(np.float32).astype(BF)
    return C


def pack_vectors(inp):
    V = {}
    g = np.asarray(inp["L0_ffn_ln_g"], np.float32)
    b = np.asarray(inp["L0_ffn_ln_b"], np.float32)
    b2 = np.asarray(inp["L0_ffn_b2"], np.float32)
    for nm, a in (("gcol", g), ("bcol", b), ("b2col", b2)):
        out = np.zeros((128, DEPTH * 8), np.float32)
        for i in range(DEPTH):
            for m in range(8):
                out[:, i * 8 + m] = a[i, m * 128:(m + 1) * 128]
        V[nm] = out
    qs = np.asarray(inp["L0_ffn_q_scale"], np.float32)
    ks = np.asarray(inp["L0_ffn_k_scale"], np.float32)
    V["qscol"] = np.concatenate([qs.T, qs.T], 0)
    V["kscol"] = np.concatenate([ks.T, ks.T], 0)
    b1 = np.asarray(inp["L0_ffn_b1"], np.float32)
    b1a = np.zeros((128, DEPTH * NCH), np.float32)
    b1g = np.zeros((128, DEPTH * NCH), np.float32)
    for i in range(DEPTH):
        for c, (off, sz) in enumerate(FFN_CHUNKS):
            b1a[:sz, i * NCH + c] = b1[i, off:off + sz]
            b1g[:sz, i * NCH + c] = b1[i, DI + off:DI + off + sz]
    V["b1acol"] = b1a
    V["b1gcol"] = b1g
    rows = np.zeros((128, 1024), np.float32)
    rows[0, :] = np.asarray(inp["L0_attn_freqs"], np.float32)
    rows[32, :] = np.arange(0, HT, dtype=np.float32)
    rows[64, :] = np.arange(HT, T, dtype=np.float32)
    V["rows2init"] = rows
    return V


ENGINES = ("tensor", "vector", "scalar", "sync", "gpsimd")


class Tick:
    __slots__ = ("eng", "seq", "needed", "value", "dma_sem", "dma_value")

    def __init__(self, eng, seq):
        self.eng = eng
        self.seq = seq
        self.needed = False
        self.value = None
        self.dma_sem = None
        self.dma_value = None


class Op:
    __slots__ = ("fn", "tick", "waits", "do_inc")

    def __init__(self, fn, tick, waits):
        self.fn = fn
        self.tick = tick
        self.waits = waits
        self.do_inc = False


class DmaSem:
    __slots__ = ("key", "count")

    def __init__(self, key):
        self.key = key
        self.count = 0


class Prog:
    def __init__(self):
        self.streams = {e: [] for e in ENGINES}
        self.n_dma_sems = 0

    def new_dma_sem(self):
        s = DmaSem(("dma", self.n_dma_sems))
        self.n_dma_sems += 1
        return s

    def _filter(self, waits):
        best_eng = {}
        best_dma = {}
        for tk in waits:
            if tk is None:
                continue
            if tk.dma_sem is not None:
                v = best_dma.get(tk.dma_sem)
                if v is None or tk.dma_value > v:
                    best_dma[tk.dma_sem] = tk.dma_value
            else:
                v = best_eng.get(tk.eng)
                if v is None or tk.seq > v.seq:
                    best_eng[tk.eng] = tk
        out = []
        for eng, tk in best_eng.items():
            tk.needed = True
            out.append(("eng", eng, tk))
        for key, val in best_dma.items():
            out.append(("dma", key, val))
        return out

    def emit(self, eng, fn, waits=()):
        tick = Tick(eng, len(self.streams[eng]))
        self.streams[eng].append(Op(fn, tick, self._filter(waits)))
        return tick

    def dma(self, queue, out_ap, in_ap, dsem, waits=(), accum=None):
        tick = Tick(queue, len(self.streams[queue]))
        dsem.count += 1
        tick.dma_sem = dsem.key
        tick.dma_value = 16 * dsem.count
        if accum is None:
            fn = lambda e, o=out_ap, i=in_ap: e.dma_start(o, i)
        else:
            fn = lambda e, o=out_ap, i=in_ap, a=accum: e.dma_start(o, i, accum_op=a)
        op = Op(fn, tick, self._filter(waits))
        op.do_inc = True
        self.streams[queue].append(op)
        return tick

    def realize(self, block, eng_sems, dma_sems):
        for ename in ENGINES:
            c = 0
            for op in self.streams[ename]:
                if op.tick.dma_sem is None and op.tick.needed:
                    c += 1
                    op.tick.value = c
                    op.do_inc = True

        def make_body(ename):
            ops = self.streams[ename]
            sem_self = eng_sems.get(ename)

            def body(eng):
                seen = {}
                for op in ops:
                    for kind, key, val in op.waits:
                        if kind == "eng":
                            v = val.value
                            sem = eng_sems[key]
                        else:
                            v = val
                            sem = dma_sems[key]
                        if seen.get(key, -1) >= v:
                            continue
                        seen[key] = v
                        eng.wait_ge(sem, v)
                    inst = op.fn(eng)
                    if op.do_inc:
                        if op.tick.dma_sem is not None:
                            inst.then_inc(dma_sems[op.tick.dma_sem], 16)
                        else:
                            inst.then_inc(sem_self, 1)
            return body

        for ename in ENGINES:
            getattr(block, ename)(make_body(ename))


class Res:
    __slots__ = ("writers", "readers")

    def __init__(self):
        self.writers = []
        self.readers = []

    def rdeps(self):
        return list(self.writers)

    def wdeps(self):
        return self.readers + self.writers

    def note_read(self, tick):
        self.readers.append(tick)

    def note_write(self, tick):
        self.writers = [tick]
        self.readers = []

    def note_write_partial(self, tick):
        self.writers.append(tick)
        if len(self.writers) > 24:
            self.writers = self.writers[-24:]


def rd(*rs):
    out = []
    for r in rs:
        out += r.rdeps()
    return out


def wr(*rs):
    out = []
    for r in rs:
        out += r.wdeps()
    return out


def build_program():
    import contextlib
    nc = bass.Bass("TRN2", target_bir_lowering=False, debug=False)
    P = Prog()
    stack = contextlib.ExitStack()
    sbuf = lambda nm, shp, dt: stack.enter_context(nc.sbuf_tensor(nm, shp, dt))
    psum = lambda nm, shp, dt: stack.enter_context(nc.psum_tensor(nm, shp, dt))
    sem = lambda nm: stack.enter_context(nc.semaphore(nm))

    x_in = nc.dram_tensor("x", [T, D], F32, kind="ExternalInput").ap()
    y_out = nc.dram_tensor("y", [T, D], F32, kind="ExternalOutput").ap()
    x2_dram = nc.dram_tensor("x2scr", [NDT, 128, T], F32, kind="Internal").ap()
    w_aWv = nc.dram_tensor("aWv", [D, D], F32, kind="ExternalInput").ap()
    w_aWo = nc.dram_tensor("aWo", [D, D], F32, kind="ExternalInput").ap()
    w_qkv = nc.dram_tensor("Wqkv", [DEPTH, D, 3 * D], F32, kind="ExternalInput").ap()
    w_wo = nc.dram_tensor("Wo", [DEPTH, D, D], F32, kind="ExternalInput").ap()
    w_w1 = nc.dram_tensor("W1", [DEPTH, D, 2 * DI], F32, kind="ExternalInput").ap()
    w_w2 = nc.dram_tensor("W2", [DEPTH, DI, D], F32, kind="ExternalInput").ap()
    dbg_drams = {}
    for nm in DEBUG_TAPS:
        dbg_drams[nm] = nc.dram_tensor("dbg_" + nm, [NDT, 128, T], F32,
                                       kind="ExternalOutput").ap()

    cst_specs = {
        "identb": ([128, 128], BF16), "identf": ([128, 128], F32),
        "pp": ([128, 128], BF16), "masks": ([128, 1536], BF16),
        "onesc": ([128, 1], BF16), "onesf": ([1, 128], F32), "epscol": ([128, 1], F32), "negpicol": ([128, 1], F32), "hsum": ([128, 128], BF16),
        "hbm": ([16, 1024], BF16),
        "csk": ([128, T], BF16), "ssk": ([128, T], BF16), "scl2": ([128, T], BF16),
        "gcol": ([128, DEPTH * 8], F32), "bcol": ([128, DEPTH * 8], F32),
        "b2col": ([128, DEPTH * 8], F32),
        "qscol": ([128, DEPTH], F32), "kscol": ([128, DEPTH], F32),
        "b1acol": ([128, DEPTH * NCH], F32), "b1gcol": ([128, DEPTH * NCH], F32),
        "rows2init": ([128, 1024], F32),
    }
    cst_dram = {nm: nc.dram_tensor("c_" + nm, list(shp), dt, kind="ExternalInput").ap()
                for nm, (shp, dt) in cst_specs.items()}

    xres = [sbuf(f"xres{i}", [128, T], F32) for i in range(NDT)]
    role = [sbuf(f"role{i}", [128, 1028], BF16) for i in range(NDT)]
    qb = [sbuf(f"qb{i}", [128, HT], BF16) for i in range(NDT)]
    kb = [sbuf(f"kb{i}", [128, HT + 256], BF16) for i in range(NDT)]
    vb = [sbuf(f"vb{i}", [128, 1024], BF16) for i in range(10)]
    arena = sbuf("arena", [128, 11 * 1024], BF16)
    A = [arena[:, i * 1024:(i + 1) * 1024] for i in range(11)]
    stg = [arena[:, 0:2048].bitcast(F32), arena[:, 2048:4096].bitcast(F32)]
    w2v = [arena[:, 8448:11264], arena[:, 5632:8448]]
    W2_SLOTMAP = [(8, 9, 10), (5, 6, 7, 8)]
    probs = [sbuf(f"probs{i}", [128, 1024], BF16) for i in range(2)]
    probsT = sbuf("probsT", [128, 1024], BF16)
    sD = [sbuf(f"sD{i}", [128, 1024], BF16) for i in range(3)]
    sD.append(vb[0])
    iostage = [sbuf("iost0", [128, 1024], F32)]
    iostage = [iostage[0], iostage[0]]
    rowscr = sbuf("rowscr", [1, 1024], F32)
    rqrowt = sbuf("rqrowt", [16, 512], F32)
    rqcolp = sbuf("rqcolp", [128, 128], F32)
    rqkb_hi = sbuf("rqkbh", [16, 512], BF16)
    rqkb_lo = sbuf("rqkbl", [16, 512], BF16)
    negmax = [sbuf(f"ngm{i}", [128, 1], F32) for i in range(2)]
    zcol = [sbuf(f"zc{i}", [128, 1], F32) for i in range(2)]
    carry = sbuf("carry", [128, 16], F32)
    csts = {nm: sbuf("s_" + nm, list(shp), dt) for nm, (shp, dt) in cst_specs.items()}

    pb = [psum(f"pb{i}", [128, 512], F32) for i in range(7)]
    pT = psum("pT", [128, 1024], BF16)

    eng_sems = {e: sem("sem_" + e) for e in ("tensor", "vector", "scalar")}
    dma_handles = {}

    def mk_dsem():
        s = P.new_dma_sem()
        dma_handles[s.key] = sem(f"dq{s.key[1]}")
        return s

    ds_cst = mk_dsem()
    ds_x = mk_dsem()
    ds_slot = [mk_dsem() for _ in range(11)]
    ds_w2 = [mk_dsem() for _ in range(2)]
    ds_x2 = mk_dsem()
    ds_x2r = mk_dsem()
    ds_out = mk_dsem()
    ds_dbg = mk_dsem()
    ds_xs = [mk_dsem(), mk_dsem()]
    ds_ys = [mk_dsem(), mk_dsem()]

    r_xres = [Res() for _ in range(NDT)]
    r_role = [Res() for _ in range(NDT)]
    r_qb = [Res() for _ in range(NDT)]
    r_kb = [Res() for _ in range(NDT)]
    r_vb = [Res() for _ in range(10)]
    r_A = [Res() for _ in range(11)]
    r_probs4 = [[Res(), Res()], [Res(), Res()]]
    r_probsT = Res()
    r_sD = [Res() for _ in range(3)]
    r_sD.append(r_vb[0])
    _rio = Res()
    r_io = [_rio, _rio]
    r_rowscr = Res()
    r_rq = Res()
    r_rqcolp = Res()
    r_rqkh = Res()
    r_rqkl = Res()
    r_negmax = [Res() for _ in range(2)]
    r_zcol = [Res() for _ in range(2)]
    r_carry = Res()
    r_pb = [Res() for _ in range(7)]
    r_pTh = [Res(), Res()]
    r_x2d = Res()

    cst_last = None
    for nm in cst_specs:
        cst_last = P.dma("sync", csts[nm][:, :], cst_dram[nm], ds_cst)
    CST = [cst_last]

    rows2 = csts["rows2init"]
    ones1 = csts["onesf"][0:1, :]
    trow = None
    freqsrow = rows2[0:1, :]
    rowMean = rowscr[0:1, 0:512]
    rowRstd = rowscr[0:1, 512:1024]
    rqrow = rqrowt[0:16, :]

    def mask_ap(which, n):
        return csts["masks"][0:128, which * 512:which * 512 + n]

    tz = P.emit("vector", lambda e: e.memset(carry[:, :], 0.0), CST)
    r_carry.note_write(tz)
    tzp = P.emit("vector", lambda e: e.memset(pT[:, :].bitcast(F32), 0.0), CST)
    r_pTh[0].note_write(tzp)
    r_pTh[1].note_write(tzp)
    for i2 in range(NDT):
        tk = P.emit("vector", lambda e, i2=i2: e.memset(kb[i2][:, 0:256], 0.0), CST)
        r_kb[i2].note_write(tk)

    def load_input():
        for tau in range(16):
            st = tau % 2
            rs = [r_A[2 * st], r_A[2 * st + 1]]
            dtick = P.dma("sync", stg[st], x_in[tau * 128:(tau + 1) * 128, :],
                          ds_xs[st], wr(*rs))
            for r in rs:
                r.note_write(dtick)
            for d in range(NDT):
                bank = (tau * NDT + d) % 4
                mm = P.emit("tensor",
                            lambda e, st=st, d=d, bank=bank:
                            e.transpose(pb[bank][:, 0:128],
                                        stg[st][:, d * 128:(d + 1) * 128],
                                        csts["identf"][:, :]),
                            [dtick] + wr(r_pb[bank]) + CST)
                r_pb[bank].note_write(mm)
                for r in rs:
                    r.note_read(mm)
                ev = P.emit("scalar",
                            lambda e, d=d, tau=tau, bank=bank:
                            e.copy(xres[d][:, tau * 128:(tau + 1) * 128], pb[bank][:, 0:128]),
                            rd(r_pb[bank]) + wr(r_xres[d]))
                r_pb[bank].note_read(ev)
                r_xres[d].note_write_partial(ev)

    def tap(stage):
        if stage not in dbg_drams:
            return
        dr = dbg_drams[stage]
        for d in range(NDT):
            tk = P.dma("sync", dr[d], xres[d][:, :], ds_dbg, rd(r_xres[d]))
            r_xres[d].note_read(tk)

    def role_copy_tile(h, shifted, k):
        role_copy_some(h, shifted, [k])

    def role_copy(h, shifted):
        role_copy_some(h, shifted, list(range(NDT)))

    def role_copy_some(h, shifted, ks):
        for k in ks:
            if shifted:
                if h == 0:
                    tz = P.emit("vector", lambda e, k=k: e.memset(role[k][:, 0:1], 0.0),
                                wr(r_role[k]))
                    r_role[k].note_write(tz)
                    tc = P.emit("vector",
                                lambda e, k=k: e.tensor_copy(role[k][:, 1:1025], xres[k][:, 0:1024]),
                                rd(r_xres[k]) + wr(r_role[k]))
                    r_role[k].note_write_partial(tc)
                else:
                    tc = P.emit("vector",
                                lambda e, k=k, h=h:
                                e.tensor_copy(role[k][:, 0:1025], xres[k][:, h * HT - 1:h * HT + 1024]),
                                rd(r_xres[k]) + wr(r_role[k]))
                    r_role[k].note_write(tc)
            else:
                tc = P.emit("vector",
                            lambda e, k=k, h=h:
                            e.tensor_copy(role[k][:, 0:1024], xres[k][:, h * HT:(h + 1) * HT]),
                            rd(r_xres[k]) + wr(r_role[k]))
                r_role[k].note_write(tc)
            r_xres[k].note_read(tc)

    def ln_stats(offs):
        for cch, bank in ((0, 4), (1, 5)):
            for k in range(NDT):
                wl = rd(r_role[k]) + CST
                if k == 0:
                    wl += wr(r_pb[bank])
                mm = P.emit("tensor",
                            lambda e, k=k, cch=cch, bank=bank, o=offs[k]:
                            e.matmul(pb[bank][0:1, :], csts["onesc"][:, 0:1],
                                     role[k][:, o + cch * 512: o + cch * 512 + 512],
                                     start=(k == 0), stop=(k == NDT - 1)),
                            wl)
                r_role[k].note_read(mm)
                r_pb[bank].note_write_partial(mm)
        for cch, bank in ((0, 6), (1, 3)):
            for k in range(NDT):
                sl = k % 2
                sq = P.emit("vector",
                            lambda e, k=k, cch=cch, sl=sl, o=offs[k]:
                            e.tensor_tensor(sD[sl][:, 0:512],
                                            role[k][:, o + cch * 512: o + cch * 512 + 512],
                                            role[k][:, o + cch * 512: o + cch * 512 + 512],
                                            ALU.mult),
                            rd(r_role[k]) + wr(r_sD[sl]))
                r_sD[sl].note_write(sq)
                r_role[k].note_read(sq)
                wl = [sq] + CST
                if k == 0:
                    wl += wr(r_pb[bank])
                mm = P.emit("tensor",
                            lambda e, k=k, bank=bank, sl=sl:
                            e.matmul(pb[bank][0:1, :], csts["onesc"][:, 0:1],
                                     sD[sl][:, 0:512], start=(k == 0), stop=(k == NDT - 1)),
                            wl)
                r_pb[bank].note_write_partial(mm)
                r_sD[sl].note_read(mm)
        banks = {0: (0, 2), 1: (1, 6)}
        for cch, (sbank, qbank) in ((0, (4, 6)), (1, (5, 3))):
            cc = cch * 512
            rbank, mbank = banks[cch]
            t1 = P.emit("vector",
                        lambda e, sbank=sbank:
                        e.tensor_scalar(rowMean[:, :], pb[sbank][0:1, :],
                                        1.0 / D, None, ALU.mult),
                        rd(r_pb[sbank]) + wr(r_rowscr))
            r_pb[sbank].note_read(t1)
            r_rowscr.note_write_partial(t1)
            tm = P.emit("tensor",
                        lambda e, mbank=mbank:
                        e.matmul(pb[mbank][:, :], ones1, rowMean[:, :],
                                 start=True, stop=True),
                        rd(r_rowscr) + wr(r_pb[mbank]) + CST)
            r_rowscr.note_read(tm)
            r_pb[mbank].note_write(tm)
            t2 = P.emit("vector",
                        lambda e, qbank=qbank:
                        e.tensor_scalar(rowRstd[:, :], pb[qbank][0:1, :],
                                        1.0 / D, None, ALU.mult),
                        rd(r_pb[qbank]) + wr(r_rowscr))
            r_pb[qbank].note_read(t2)
            r_rowscr.note_write_partial(t2)
            t3 = P.emit("vector",
                        lambda e:
                        e.tensor_tensor(rowMean[:, :], rowMean[:, :],
                                        rowMean[:, :], ALU.mult),
                        [tm] + rd(r_rowscr))
            r_rowscr.note_write_partial(t3)
            t4 = P.emit("vector",
                        lambda e:
                        e.tensor_tensor(rowRstd[:, :], rowRstd[:, :],
                                        rowMean[:, :], ALU.subtract),
                        rd(r_rowscr))
            r_rowscr.note_write_partial(t4)
            t5 = P.emit("scalar",
                        lambda e:
                        e.activation(rowRstd[:, :], rowRstd[:, :],
                                     AF.Sqrt, bias=csts["epscol"][0:1, 0:1]),
                        rd(r_rowscr))
            r_rowscr.note_write_partial(t5)
            t6 = P.emit("vector",
                        lambda e:
                        e.reciprocal(rowRstd[:, :], rowRstd[:, :]),
                        rd(r_rowscr))
            r_rowscr.note_write_partial(t6)
            tb = P.emit("tensor",
                        lambda e, rbank=rbank:
                        e.matmul(pb[rbank][:, :], ones1, rowRstd[:, :],
                                 start=True, stop=True),
                        rd(r_rowscr) + wr(r_pb[rbank]) + CST)
            r_rowscr.note_read(tb)
            r_pb[rbank].note_write(tb)
        return banks

    def ln_apply_role(offs, banks, layer):
        for m in range(NDT):
            for cch in range(2):
                rbank, mbank = banks[cch]
                cc = offs[m] + cch * 512
                sl = m % 2
                t1 = P.emit("vector",
                            lambda e, m=m, cc=cc, mbank=mbank, sl=sl:
                            e.tensor_tensor(sD[sl][:, 0:512], role[m][:, cc:cc + 512],
                                            pb[mbank][:, :], ALU.subtract),
                            rd(r_role[m], r_pb[mbank]) + wr(r_sD[sl]))
                r_role[m].note_read(t1)
                r_pb[mbank].note_read(t1)
                r_sD[sl].note_write(t1)
                if layer is None:
                    t2 = P.emit("vector",
                                lambda e, m=m, cc=cc, rbank=rbank, sl=sl:
                                e.tensor_tensor(role[m][:, cc:cc + 512], sD[sl][:, 0:512],
                                                pb[rbank][:, :], ALU.mult),
                                rd(r_sD[sl], r_pb[rbank]) + wr(r_role[m]))
                    r_sD[sl].note_read(t2)
                    r_pb[rbank].note_read(t2)
                    r_role[m].note_write_partial(t2)
                else:
                    t2 = P.emit("vector",
                                lambda e, cc=cc, rbank=rbank, sl=sl:
                                e.tensor_tensor(sD[sl][:, 0:512], sD[sl][:, 0:512],
                                                pb[rbank][:, :], ALU.mult),
                                rd(r_sD[sl], r_pb[rbank]))
                    r_pb[rbank].note_read(t2)
                    r_sD[sl].note_write(t2)
                    gc = layer * 8 + m
                    t3 = P.emit("vector",
                                lambda e, m=m, cc=cc, gc=gc, sl=sl:
                                e.tensor_scalar(role[m][:, cc:cc + 512], sD[sl][:, 0:512],
                                                csts["gcol"][:, gc:gc + 1],
                                                csts["bcol"][:, gc:gc + 1],
                                                ALU.mult, ALU.add),
                                rd(r_sD[sl]) + wr(r_role[m]) + CST)
                    r_sD[sl].note_read(t3)
                    r_role[m].note_write_partial(t3)

    def ln_apply_xres(h, banks):
        for m in range(NDT):
            for cch in range(2):
                rbank, mbank = banks[cch]
                cc = h * HT + cch * 512
                t1 = P.emit("vector",
                            lambda e, m=m, cc=cc, mbank=mbank:
                            e.tensor_tensor(xres[m][:, cc:cc + 512], xres[m][:, cc:cc + 512],
                                            pb[mbank][:, :], ALU.subtract),
                            rd(r_pb[mbank]) + wr(r_xres[m]))
                r_pb[mbank].note_read(t1)
                r_xres[m].note_write_partial(t1)
                t2 = P.emit("vector",
                            lambda e, m=m, cc=cc, rbank=rbank:
                            e.tensor_tensor(xres[m][:, cc:cc + 512], xres[m][:, cc:cc + 512],
                                            pb[rbank][:, :], ALU.mult),
                            rd(r_pb[rbank]) + wr(r_xres[m]))
                r_pb[rbank].note_read(t2)
                r_xres[m].note_write_partial(t2)

    def load_rot(src, slot, three_d=False):
        dst = A[slot].rearrange("p (k m) -> p k m", k=8) if three_d else A[slot][:, :]
        tk = P.dma("gpsimd", dst, src, ds_slot[slot], wr(r_A[slot]))
        r_A[slot].note_write(tk)
        return tk

    def load_w1(i, ch, gate, slot):
        off, sz = FFN_CHUNKS[ch]
        c0 = off + (DI if gate else 0)
        src = w_w1[i, :, c0:c0 + sz].rearrange("(kc p) m -> p kc m", p=128)
        dst = A[slot].rearrange("p (k m) -> p k m", k=8)[:, :, 0:sz]
        tk = P.dma("gpsimd", dst, src, ds_slot[slot], wr(r_A[slot]))
        r_A[slot].note_write(tk)
        return tk

    def load_w2(i, mo, j):
        slots = [r_A[s] for s in W2_SLOTMAP[j]]
        src = w_w2[i, 0:2688, mo * 128:(mo + 1) * 128].rearrange("(c p) m -> p c m", p=128)
        dstw2 = w2v[j][:, 0:2688].rearrange("p (c m) -> p c m", c=21)
        tk1 = P.dma("gpsimd", dstw2, src, ds_w2[j], wr(*slots))
        src2 = w_w2[i, 2688:2730, mo * 128:(mo + 1) * 128]
        tk2 = P.dma("gpsimd", w2v[j][0:42, 2688:2816], src2, ds_w2[j])
        for s in W2_SLOTMAP[j]:
            r_A[s].note_write(tk1)
            r_A[s].note_write_partial(tk2)
        return [tk1, tk2]

    SHIFT_OFFS = [1, 1, 1, 1, 0, 0, 0, 0]

    def qkv_qk(i, h):
        offs = SHIFT_OFFS
        wts = {}
        def ld(m):
            if m < 16 and m not in wts:
                wts[m] = load_rot(w_qkv[i, :, m * 128:(m + 1) * 128]
                                  .rearrange("(kc p) m -> p kc m", p=128), m % 3,
                                  three_d=True)
        ld(0); ld(1)
        for m in range(16):
            slot = m % 3
            ld(m + 1)
            wt = wts[m]
            dstbuf, dres = (qb, r_qb) if m < 8 else (kb, r_kb)
            mi = m % 8
            col0 = 0 if m < 8 else 256
            for k in range(NDT):
                for n in range(2):
                    bank = (m * 2 + n) % 4
                    wl = [wt] + rd(r_role[k]) + CST
                    if k == 0:
                        wl += wr(r_pb[bank])
                    mm = P.emit("tensor",
                                lambda e, slot=slot, k=k, n=n, bank=bank, o=offs[k],
                                       s=(k == 0), p=(k == NDT - 1):
                                e.matmul(pb[bank][:, :], A[slot][:, k * 128:(k + 1) * 128],
                                         role[k][:, o + n * 512: o + n * 512 + 512],
                                         start=s, stop=p),
                                wl)
                    r_A[slot].note_read(mm)
                    r_role[k].note_read(mm)
                    r_pb[bank].note_write_partial(mm)
            for n in range(2):
                bank = (m * 2 + n) % 4
                ev = P.emit("scalar",
                            lambda e, mi=mi, n=n, bank=bank, dstbuf=dstbuf, col0=col0:
                            e.copy(dstbuf[mi][:, col0 + n * 512: col0 + n * 512 + 512],
                                   pb[bank][:, :]),
                            rd(r_pb[bank]) + wr(dres[mi]))
                r_pb[bank].note_read(ev)
                dres[mi].note_write_partial(ev)
    def qkv_v(i, h):
        offs = SHIFT_OFFS
        vt = []
        for kc in range(NDT):
            slot = 3 + kc
            tk = P.dma("gpsimd", A[slot][:, :], w_qkv[i, kc * 128:(kc + 1) * 128, 2048:3072],
                       ds_slot[slot], wr(r_A[slot]))
            r_A[slot].note_write(tk)
            vt.append(tk)
        for tloc in range(8):
            tau = h * 8 + tloc
            vslot = tau % 10
            for n in range(2):
                bank = (tloc * 2 + n) % 4
                for k in range(NDT):
                    wl = [vt[k]] + rd(r_role[k]) + CST
                    if k == 0:
                        wl += wr(r_pb[bank])
                    mm = P.emit("tensor",
                                lambda e, k=k, n=n, bank=bank, tloc=tloc, o=offs[k],
                                       s=(k == 0), p=(k == NDT - 1):
                                e.matmul(pb[bank][:, :],
                                         role[k][:, o + tloc * 128: o + tloc * 128 + 128],
                                         A[3 + k][:, n * 512:(n + 1) * 512],
                                         start=s, stop=p),
                                wl)
                    r_A[3 + k].note_read(mm)
                    r_role[k].note_read(mm)
                    r_pb[bank].note_write_partial(mm)
                ev = P.emit("scalar",
                            lambda e, vslot=vslot, n=n, bank=bank:
                            e.copy(vb[vslot][:, n * 512:(n + 1) * 512], pb[bank][:, :]),
                            rd(r_pb[bank]) + wr(r_vb[vslot]))
                r_pb[bank].note_read(ev)
                r_vb[vslot].note_write_partial(ev)

    def qk_prep(i, h, which, cch):
        buf, rbuf = (qb, r_qb) if which == 0 else (kb, r_kb)
        col0 = 0 if which == 0 else 256
        sccol = csts["qscol"] if which == 0 else csts["kscol"]
        if True:
            bank = 4
            for m in range(NDT):
                sl = m % 2
                sq = P.emit("vector",
                            lambda e, m=m, cch=cch, sl=sl, buf=buf, col0=col0:
                            e.tensor_tensor(sD[sl][:, 0:512],
                                            buf[m][:, col0 + cch * 512: col0 + cch * 512 + 512],
                                            buf[m][:, col0 + cch * 512: col0 + cch * 512 + 512],
                                            ALU.mult),
                            rd(rbuf[m]) + wr(r_sD[sl]))
                r_sD[sl].note_write(sq)
                rbuf[m].note_read(sq)
                wl = [sq] + CST
                if m == 0:
                    wl += wr(r_pb[bank])
                mm = P.emit("tensor",
                            lambda e, m=m, bank=bank, sl=sl, s=(m == 0), p=(m == NDT - 1):
                            e.matmul(pb[bank][0:16, :], csts["hsum"][:, m * 16:(m + 1) * 16],
                                     sD[sl][:, 0:512], start=s, stop=p),
                            wl)
                r_pb[bank].note_write_partial(mm)
                r_sD[sl].note_read(mm)
            t1 = P.emit("scalar",
                        lambda e, bank=bank:
                        e.activation(rqrow[:, :], pb[bank][0:16, :], AF.Sqrt),
                        rd(r_pb[bank]) + wr(r_rq))
            r_pb[bank].note_read(t1)
            r_rq.note_write(t1)
            t2 = P.emit("vector",
                        lambda e:
                        e.tensor_scalar(rqrow[:, :], rqrow[:, :],
                                        1e-12, None, ALU.max),
                        rd(r_rq))
            r_rq.note_write_partial(t2)
            t3 = P.emit("vector",
                        lambda e:
                        e.reciprocal(rqrow[:, :], rqrow[:, :]),
                        rd(r_rq))
            r_rq.note_write_partial(t3)
        if which == 0:
            for tch in range(4):
                bankT = 6 if tch % 2 == 0 else 3
                tt = P.emit("tensor",
                            lambda e, tch=tch, bankT=bankT:
                            e.transpose(pb[bankT][:, 0:16], rqrow[0:16, tch * 128:(tch + 1) * 128],
                                        csts["identf"][0:16, 0:16]),
                            rd(r_rq) + wr(r_pb[bankT]) + CST)
                r_rq.note_read(tt)
                r_pb[bankT].note_write(tt)
                ev = P.emit("scalar",
                            lambda e, tch=tch, cch=cch, bankT=bankT:
                            e.copy(rqcolp[:, cch * 64 + tch * 16: cch * 64 + tch * 16 + 16],
                                   pb[bankT][:, 0:16]),
                            rd(r_pb[bankT]) + wr(r_rqcolp))
                r_pb[bankT].note_read(ev)
                r_rqcolp.note_write_partial(ev)
        else:
            thi = P.emit("vector",
                         lambda e: e.tensor_copy(rqkb_hi[:, :], rqrow[:, :]),
                         rd(r_rq) + wr(r_rqkh))
            r_rq.note_read(thi)
            r_rqkh.note_write(thi)
            tlo = P.emit("vector",
                         lambda e: e.tensor_tensor(rqrow[:, :], rqrow[:, :],
                                                   rqkb_hi[:, :], ALU.subtract),
                         rd(r_rq, r_rqkh))
            r_rq.note_write_partial(tlo)
            tlo2 = P.emit("vector",
                          lambda e: e.tensor_copy(rqkb_lo[:, :], rqrow[:, :]),
                          rd(r_rq) + wr(r_rqkl))
            r_rq.note_read(tlo2)
            r_rqkl.note_write(tlo2)
        cc0 = col0 + cch * 512
        tcols0 = h * HT + cch * 512
        uticks = {}

        def stageA(m):
            sl = m % 2
            uoff = sl * 512
            if which == 1:
                tb = P.emit("tensor",
                            lambda e, m=m:
                            e.matmul(pb[6][:, :], csts["hbm"][0:16, m * 128:(m + 1) * 128],
                                     rqkb_hi[:, :], start=True, stop=False),
                            rd(r_rqkh) + wr(r_pb[6]) + CST)
                r_rqkh.note_read(tb)
                r_pb[6].note_write(tb)
                tb2 = P.emit("tensor",
                             lambda e, m=m:
                             e.matmul(pb[6][:, :], csts["hbm"][0:16, m * 128:(m + 1) * 128],
                                      rqkb_lo[:, :], start=False, stop=True),
                             rd(r_rqkl) + CST)
                r_rqkl.note_read(tb2)
                r_pb[6].note_write_partial(tb2)
                th = P.emit("vector",
                            lambda e, m=m, buf=buf, sccol=sccol, i=i, sl=sl:
                            e.scalar_tensor_tensor(sD[sl][:, 0:512], buf[m][:, cc0:cc0 + 512],
                                                   sccol[:, i:i + 1], pb[6][:, :],
                                                   ALU.mult, ALU.mult),
                            rd(rbuf[m], r_pb[6]) + wr(r_sD[sl]) + CST)
                rbuf[m].note_read(th)
                r_pb[6].note_read(th)
                r_sD[sl].note_write(th)
                src_half = slice(0, 512)
            else:
                th = P.emit("vector",
                            lambda e, m=m, buf=buf, sccol=sccol, i=i, sl=sl:
                            e.tensor_scalar(sD[sl][:, 0:512], buf[m][:, cc0:cc0 + 512],
                                            sccol[:, i:i + 1], None, ALU.mult),
                            rd(rbuf[m]) + wr(r_sD[sl]) + CST)
                rbuf[m].note_read(th)
                r_sD[sl].note_write(th)
                th2 = P.emit("vector",
                             lambda e, sl=sl:
                             e.tensor_tensor(sD[sl][:, 512:1024], sD[sl][:, 0:512],
                                             csts["scl2"][:, tcols0:tcols0 + 512], ALU.mult),
                             rd(r_sD[sl]) + CST)
                r_sD[sl].note_write_partial(th2)
                src_half = slice(512, 1024)
            tu = P.emit("vector",
                        lambda e, sl=sl, src_half=src_half, uoff=uoff:
                        e.tensor_tensor(sD[2][:, uoff:uoff + 512], sD[sl][:, src_half],
                                        csts["ssk"][:, tcols0:tcols0 + 512], ALU.mult),
                        rd(r_sD[sl]) + wr(r_sD[2]) + CST)
            r_sD[sl].note_read(tu)
            r_sD[2].note_write_partial(tu)
            uticks[m] = (tu, src_half)

        def stageB(m):
            sl = m % 2
            uoff = sl * 512
            tu, src_half = uticks[m]
            tw = P.emit("tensor",
                        lambda e, uoff=uoff:
                        e.matmul(pb[3][:, :], csts["pp"][:, :], sD[2][:, uoff:uoff + 512],
                                 start=True, stop=True),
                        [tu] + wr(r_pb[3]) + CST)
            r_sD[2].note_read(tw)
            r_pb[3].note_write(tw)
            ta = P.emit("vector",
                        lambda e, sl=sl, src_half=src_half, uoff=uoff:
                        e.tensor_tensor(sD[2][:, uoff:uoff + 512], sD[sl][:, src_half],
                                        csts["csk"][:, tcols0:tcols0 + 512], ALU.mult),
                        rd(r_sD[sl]) + [tw] + CST)
            r_sD[sl].note_read(ta)
            r_sD[2].note_write_partial(ta)
            tf = P.emit("vector",
                        lambda e, m=m, buf=buf, uoff=uoff:
                        e.tensor_tensor(buf[m][:, cc0:cc0 + 512], sD[2][:, uoff:uoff + 512],
                                        pb[3][:, :], ALU.add),
                        rd(r_sD[2], r_pb[3]) + wr(rbuf[m]))
            r_sD[2].note_read(tf)
            r_pb[3].note_read(tf)
            rbuf[m].note_write_partial(tf)

        for m in range(NDT):
            stageA(m)
            if m > 0:
                stageB(m - 1)
        stageB(NDT - 1)

    def attention(i, h, ws):
        its = [(w, hd) for w in ws for hd in range(NH)]

        def stage_scores(idx):
            w, hd = its[idx]
            m = hd // 2
            p0 = (hd % 2) * 64
            wloc = w % 4
            par = idx % 2
            for qc in range(2):
                bank = 2 * par + qc
                qcol = wloc * 256 + qc * 128
                kcol = wloc * 256 + qc * 128
                maskid = (1 + qc) if w == 0 else 0
                nn = 384 if (wloc == 3 and qc == 1) else 512
                wl = rd(r_qb[m], r_kb[m]) + wr(r_pb[bank]) + CST
                mm = P.emit("tensor",
                            lambda e, m=m, p0=p0, qcol=qcol, kcol=kcol, bank=bank, nn=nn:
                            e.matmul(pb[bank][:, 0:nn], qb[m][p0:p0 + 64, qcol:qcol + 128],
                                     kb[m][p0:p0 + 64, kcol:kcol + nn],
                                     start=True, stop=False),
                            wl)
                r_qb[m].note_read(mm)
                r_kb[m].note_read(mm)
                r_pb[bank].note_write(mm)
                mk = P.emit("tensor",
                            lambda e, bank=bank, maskid=maskid, nn=nn:
                            e.matmul(pb[bank][:, 0:nn], csts["identb"][:, :],
                                     mask_ap(maskid, nn), start=False, stop=True),
                            CST)
                r_pb[bank].note_write_partial(mk)

        def stage_softmax(idx):
            par = idx % 2
            w, hd = its[idx]
            wloc = w % 4
            for qc in range(2):
                bank = 2 * par + qc
                poff = par * 512
                rpr = r_probs4[qc][par]
                tch = wloc * 2 + qc
                rqc = (tch // 4) * 64 + (tch % 4) * 16 + hd
                tmx = P.emit("vector",
                             lambda e, bank=bank, qc=qc:
                             e.reduce_max(negmax[qc][:, 0:1], pb[bank][:, 0:384],
                                          axis=AX.X, negate=True),
                             rd(r_pb[bank]) + wr(r_negmax[qc]))
                r_pb[bank].note_read(tmx)
                r_negmax[qc].note_write(tmx)
                tbm = P.emit("vector",
                             lambda e, qc=qc, rqc=rqc:
                             e.tensor_tensor(negmax[qc][:, 0:1], negmax[qc][:, 0:1],
                                             rqcolp[:, rqc:rqc + 1], ALU.mult),
                             rd(r_negmax[qc], r_rqcolp))
                r_rqcolp.note_read(tbm)
                r_negmax[qc].note_write(tbm)
                tex = P.emit("scalar",
                             lambda e, bank=bank, qc=qc, rqc=rqc, poff=poff:
                             e.activation(probs[qc][:, poff:poff + 384], pb[bank][:, 0:384],
                                          AF.Exp, bias=negmax[qc][:, 0:1],
                                          scale=rqcolp[:, rqc:rqc + 1],
                                          accum_out=zcol[qc][:, 0:1]),
                             [tbm] + wr(rpr, r_zcol[qc]) + rd(r_rqcolp))
                r_pb[bank].note_read(tex)
                r_negmax[qc].note_read(tex)
                rpr.note_write(tex)
                r_zcol[qc].note_write(tex)
                trz = P.emit("vector",
                             lambda e, qc=qc:
                             e.reciprocal(zcol[qc][:, 0:1], zcol[qc][:, 0:1]),
                             rd(r_zcol[qc]))
                r_zcol[qc].note_write(trz)
                tsc = P.emit("vector",
                             lambda e, qc=qc, poff=poff:
                             e.tensor_scalar(probs[qc][:, poff:poff + 384],
                                             probs[qc][:, poff:poff + 384],
                                             zcol[qc][:, 0:1], None, ALU.mult),
                             rd(rpr, r_zcol[qc]))
                r_zcol[qc].note_read(tsc)
                rpr.note_write(tsc)

        pT_slots = [(probsT[:, :], r_probsT), (A[3][:, 0:1024], r_A[3])]

        PT_COLS = {(0, 0): 0, (0, 1): 128, (0, 2): 384,
                   (1, 0): 256, (1, 1): 512, (1, 2): 640}

        def stage_transp(idx):
            par = idx % 2
            poff = par * 512
            for qc in range(2):
                rpr = r_probs4[qc][par]
                for c in range(3):
                    col = PT_COLS[(qc, c)]
                    hh = col // 512
                    wl = rd(rpr) + CST
                    if qc == 0 and c == 0:
                        wl += wr(r_pTh[0]) + wr(r_pTh[1])
                    tt = P.emit("tensor",
                                lambda e, qc=qc, c=c, col=col, poff=poff:
                                e.transpose(pT[:, col:col + 128],
                                            probs[qc][:, poff + c * 128: poff + (c + 1) * 128],
                                            csts["identb"][:, :]),
                                wl)
                    rpr.note_read(tt)
                    r_pTh[hh].note_write_partial(tt)
            pts, ptres = pT_slots[idx % 2]
            tev = P.emit("scalar",
                         lambda e, pts=pts: e.copy(pts[:, 0:768], pT[:, 0:768]),
                         rd(r_pTh[0], r_pTh[1]) + wr(ptres))
            r_pTh[0].note_read(tev)
            r_pTh[1].note_read(tev)
            ptres.note_write(tev)

        def stage_av(idx):
            w, hd = its[idx]
            m = hd // 2
            p0 = (hd % 2) * 64
            wloc = w % 4
            pts, ptres = pT_slots[idx % 2]
            avbank = 4 + (hd // 2) % 2
            first = True
            for jt in (1, 0, 2, 3):
                tau = 2 * w - 2 + jt
                if tau < 0:
                    continue
                vslot = tau % 10
                if jt == 0:
                    oc0, oc1, pc0, pcn = 0, 128, 0, 128
                elif jt == 1:
                    oc0, oc1, pc0, pcn = 0, 256, 128, 256
                elif jt == 2:
                    oc0, oc1, pc0, pcn = 0, 256, 384, 256
                else:
                    oc0, oc1, pc0, pcn = 128, 256, 640, 128
                wl = rd(r_vb[vslot]) + rd(ptres)
                if first and p0 == 0:
                    wl += wr(r_pb[avbank])
                mm = P.emit("tensor",
                            lambda e, vslot=vslot, avbank=avbank, p0=p0, hd=hd,
                                   pts=pts, s=first, p=(jt == 3),
                                   oc0=oc0, oc1=oc1, pc0=pc0, pcn=pcn:
                            e.matmul(pb[avbank][p0:p0 + 64, oc0:oc1],
                                     vb[vslot][:, hd * 64:(hd + 1) * 64],
                                     pts[:, pc0:pc0 + pcn],
                                     start=s, stop=p),
                            wl)
                first = False
                r_vb[vslot].note_read(mm)
                ptres.note_read(mm)
                r_pb[avbank].note_write_partial(mm)
            if hd % 2 == 1:
                ev = P.emit("vector",
                            lambda e, m=m, wloc=wloc, avbank=avbank:
                            e.tensor_copy(qb[m][:, wloc * 256:(wloc + 1) * 256],
                                          pb[avbank][:, 0:256]),
                            rd(r_pb[avbank]) + wr(r_qb[m]))
                r_pb[avbank].note_read(ev)
                r_qb[m].note_write_partial(ev)

        n = len(its)
        for idx in range(n):
            stage_scores(idx)
            if idx > 1:
                stage_transp(idx - 2)
            if idx > 2:
                stage_av(idx - 3)
            stage_softmax(idx)
        stage_transp(n - 2)
        stage_av(n - 3)
        stage_transp(n - 1)
        stage_av(n - 2)
        stage_av(n - 1)

    def wo_phase(i, h):
        wts = {}
        def ld(m):
            if m < NDT and m not in wts:
                wts[m] = load_rot(w_wo[i, :, m * 128:(m + 1) * 128]
                                  .rearrange("(kc p) m -> p kc m", p=128), m % 3,
                                  three_d=True)
        ld(0); ld(1)
        for m in range(NDT):
            slot = m % 3
            ld(m + 1)
            wt = wts[m]
            for k in range(NDT):
                for n in range(2):
                    bank = (m * 2 + n) % 4
                    wl = [wt] + rd(r_qb[k]) + CST
                    if k == 0:
                        wl += wr(r_pb[bank])
                    mm = P.emit("tensor",
                                lambda e, slot=slot, k=k, n=n, bank=bank,
                                       s=(k == 0), p=(k == NDT - 1):
                                e.matmul(pb[bank][:, :], A[slot][:, k * 128:(k + 1) * 128],
                                         qb[k][:, n * 512:(n + 1) * 512],
                                         start=s, stop=p),
                                wl)
                    r_A[slot].note_read(mm)
                    r_qb[k].note_read(mm)
                    r_pb[bank].note_write_partial(mm)
            for n in range(2):
                bank = (m * 2 + n) % 4
                ev = P.emit("vector",
                            lambda e, m=m, n=n, h=h, bank=bank:
                            e.tensor_tensor(xres[m][:, h * HT + n * 512: h * HT + n * 512 + 512],
                                            xres[m][:, h * HT + n * 512: h * HT + n * 512 + 512],
                                            pb[bank][:, :], ALU.add),
                            rd(r_pb[bank]) + wr(r_xres[m]))
                r_pb[bank].note_read(ev)
                r_xres[m].note_write_partial(ev)

    def ffn_phase(i, h, next_role=None):
        offs = SHIFT_OFFS
        ag_tiles = []
        for p in range(NCH):
            if p < 8:
                ag_tiles.append((qb[p], r_qb[p], 0))
            elif p < 16:
                ag_tiles.append((kb[p - 8], r_kb[p - 8], 256))
            else:
                ag_tiles.append((vb[p - 16], r_vb[p - 16], 0))
        w1ld = {}
        def ldw1(p):
            if p < NCH and p not in w1ld:
                w1ld[p] = (load_w1(i, p, False, (2 * p) % 5),
                           load_w1(i, p, True, (2 * p + 1) % 5))
        ldw1(0); ldw1(1)
        for p in range(NCH):
            off, sz = FFN_CHUNKS[p]
            sa = (2 * p) % 5
            sg = (2 * p + 1) % 5
            ta, tg = w1ld[p]
            for n in range(2):
                banka, bankg = n, 2 + n
                for k in range(NDT):
                    wl = [ta] + rd(r_role[k]) + CST
                    if k == 0:
                        wl += wr(r_pb[banka])
                    mm = P.emit("tensor",
                                lambda e, sa=sa, k=k, n=n, banka=banka, sz=sz, o=offs[k],
                                       s=(k == 0), q=(k == NDT - 1):
                                e.matmul(pb[banka][0:sz, :], A[sa][:, k * 128:k * 128 + sz],
                                         role[k][:, o + n * 512: o + n * 512 + 512],
                                         start=s, stop=q),
                                wl)
                    r_A[sa].note_read(mm)
                    r_role[k].note_read(mm)
                    r_pb[banka].note_write_partial(mm)
                for k in range(NDT):
                    wl = [tg] + rd(r_role[k]) + CST
                    if k == 0:
                        wl += wr(r_pb[bankg])
                    mm = P.emit("tensor",
                                lambda e, sg=sg, k=k, n=n, bankg=bankg, sz=sz, o=offs[k],
                                       s=(k == 0), q=(k == NDT - 1):
                                e.matmul(pb[bankg][0:sz, :], A[sg][:, k * 128:k * 128 + sz],
                                         role[k][:, o + n * 512: o + n * 512 + 512],
                                         start=s, stop=q),
                                wl)
                    r_A[sg].note_read(mm)
                    r_role[k].note_read(mm)
                    r_pb[bankg].note_write_partial(mm)
                bc = i * NCH + p
                gslot = n
                tge = P.emit("scalar",
                             lambda e, bankg=bankg, gslot=gslot, sz=sz, bc=bc:
                             e.activation(probs[gslot][0:sz, 0:512], pb[bankg][0:sz, :], AF.Gelu,
                                          bias=csts["b1gcol"][0:sz, bc:bc + 1]),
                             rd(r_pb[bankg]) + wr(r_probs4[gslot][0]) + CST)
                r_pb[bankg].note_read(tge)
                r_probs4[gslot][0].note_write(tge)
                agbuf, agres, agcol = ag_tiles[p]
                tag = P.emit("vector",
                             lambda e, banka=banka, gslot=gslot, sz=sz, bc=bc,
                                    agbuf=agbuf, agcol=agcol, n=n:
                             e.scalar_tensor_tensor(
                                 agbuf[0:sz, agcol + n * 512: agcol + n * 512 + 512],
                                 pb[banka][0:sz, :], csts["b1acol"][0:sz, bc:bc + 1],
                                 probs[gslot][0:sz, 0:512], ALU.add, ALU.mult),
                             rd(r_pb[banka], r_probs4[gslot][0]) + wr(agres) + CST)
                r_pb[banka].note_read(tag)
                r_probs4[gslot][0].note_read(tag)
                agres.note_write_partial(tag)
            ldw1(p + 2)
        w2ld = {}
        def ldw2(mo):
            if mo < NDT and mo not in w2ld:
                w2ld[mo] = load_w2(i, mo, mo % 2)
        ldw2(0); ldw2(1)
        for mo in range(NDT):
            j = mo % 2
            ldw2(mo + 1)
            wts = w2ld[mo]
            for n in range(2):
                bank = 4 + n
                for p in range(NCH):
                    off, sz = FFN_CHUNKS[p]
                    agbuf, agres, agcol = ag_tiles[p]
                    wl = wts + rd(agres)
                    if p == 0:
                        wl += wr(r_pb[bank])
                    mm = P.emit("tensor",
                                lambda e, j=j, p=p, n=n, bank=bank, sz=sz,
                                       agbuf=agbuf, agcol=agcol,
                                       s=(p == 0), q=(p == NCH - 1):
                                e.matmul(pb[bank][:, :], w2v[j][0:sz, p * 128:(p + 1) * 128],
                                         agbuf[0:sz, agcol + n * 512: agcol + n * 512 + 512],
                                         start=s, stop=q),
                                wl)
                    for s2 in W2_SLOTMAP[j]:
                        r_A[s2].note_read(mm)
                    agres.note_read(mm)
                    r_pb[bank].note_write_partial(mm)
                gc = i * 8 + mo
                ev = P.emit("vector",
                            lambda e, mo=mo, n=n, h=h, bank=bank, gc=gc:
                            e.scalar_tensor_tensor(
                                xres[mo][:, h * HT + n * 512: h * HT + n * 512 + 512],
                                pb[bank][:, :], csts["b2col"][:, gc:gc + 1],
                                xres[mo][:, h * HT + n * 512: h * HT + n * 512 + 512],
                                ALU.add, ALU.add),
                            rd(r_pb[bank]) + wr(r_xres[mo]) + CST)
                r_pb[bank].note_read(ev)
                r_xres[mo].note_write_partial(ev)
            if next_role is not None:
                role_copy_tile(next_role[0], next_role[1], mo)

    def transpose_weight_to_wv(src_dram):
        for m in range(NDT):
            slot = m % 3
            tk = load_rot(src_dram[m * 128:(m + 1) * 128, :], slot)
            for d in range(NDT):
                wl = [tk] + CST
                if d == 0:
                    wl += wr(r_pTh[0]) + wr(r_pTh[1])
                tt = P.emit("tensor",
                            lambda e, slot=slot, d=d:
                            e.transpose(pT[:, d * 128:(d + 1) * 128],
                                        A[slot][:, d * 128:(d + 1) * 128],
                                        csts["identb"][:, :]),
                            wl)
                r_A[slot].note_read(tt)
                r_pTh[d // 4].note_write_partial(tt)
            for d in range(NDT):
                ev = P.emit("scalar",
                            lambda e, d=d, m=m:
                            e.copy(A[3 + d][:, m * 128:(m + 1) * 128],
                                   pT[:, d * 128:(d + 1) * 128]),
                            rd(r_pTh[0], r_pTh[1]) + wr(r_A[3 + d]))
                r_pTh[0].note_read(ev)
                r_pTh[1].note_read(ev)
                r_A[3 + d].note_write_partial(ev)

    def outer_attention():
        for h in range(2):
            role_copy(h, shifted=False)
            banks = ln_stats([0] * NDT)
            ln_apply_role([0] * NDT, banks, layer=None)
            transpose_weight_to_wv(w_aWv)
            for m in range(NDT):
                for n in range(2):
                    bank = (m * 2 + n) % 4
                    for k in range(NDT):
                        wl = rd(r_A[3 + k], r_role[k]) + CST
                        if k == 0:
                            wl += wr(r_pb[bank])
                        mm = P.emit("tensor",
                                    lambda e, k=k, m=m, n=n, bank=bank,
                                           s=(k == 0), p=(k == NDT - 1):
                                    e.matmul(pb[bank][:, :], A[3 + k][:, m * 128:(m + 1) * 128],
                                             role[k][:, n * 512:(n + 1) * 512],
                                             start=s, stop=p),
                                    wl)
                        r_A[3 + k].note_read(mm)
                        r_role[k].note_read(mm)
                        r_pb[bank].note_write_partial(mm)
                    ev = P.emit("scalar",
                                lambda e, m=m, n=n, bank=bank:
                                e.copy(qb[m][:, n * 512:(n + 1) * 512], pb[bank][:, :]),
                                rd(r_pb[bank]) + wr(r_qb[m]))
                    r_pb[bank].note_read(ev)
                    r_qb[m].note_write_partial(ev)
            for n in range(2):
                tst = P.emit("vector",
                             lambda e, h=h, n=n:
                             e.tensor_copy(rowscr[0:1, n * 512:(n + 1) * 512],
                                           rows2[32 + 32 * h:33 + 32 * h,
                                                 n * 512:(n + 1) * 512]),
                             CST + wr(r_rowscr))
                r_rowscr.note_write_partial(tst)
            for d in range(NDT):
                for n in range(2):
                    bank = 4 + n
                    tp = P.emit("tensor",
                                lambda e, d=d, n=n, h=h, bank=bank:
                                e.matmul(pb[bank][:, :],
                                         freqsrow[0:1, d * 128:(d + 1) * 128],
                                         rowscr[0:1, n * 512:(n + 1) * 512],
                                         start=True, stop=True),
                                CST + rd(r_rowscr) + wr(r_pb[bank]))
                    r_pb[bank].note_write(tp)
                    ioA = iostage[0][:, 0:512]
                    ioB = iostage[0][:, 512:1024]
                    ioBint = ioB.bitcast(mybir.dt.int32)
                    for off, dst, rdst in ((0.0, sD[0], r_sD[0]),
                                           (0.5 * PI, sD[1], r_sD[1])):
                        t1 = P.emit("vector",
                                    lambda e, bank=bank, off=off:
                                    e.tensor_scalar(ioA, pb[bank][:, :], off,
                                                    1.0 / (2 * PI), ALU.add, ALU.mult),
                                    rd(r_pb[bank]) + [tp] + wr(r_io[0]))
                        r_pb[bank].note_read(t1)
                        r_io[0].note_write(t1)
                        t2 = P.emit("vector",
                                    lambda e: e.tensor_copy(ioBint, ioA),
                                    rd(r_io[0]))
                        r_io[0].note_write_partial(t2)
                        t3 = P.emit("vector",
                                    lambda e: e.tensor_tensor(ioA, ioA, ioBint,
                                                              ALU.subtract),
                                    rd(r_io[0]))
                        r_io[0].note_write_partial(t3)
                        t4 = P.emit("vector",
                                    lambda e: e.tensor_scalar(ioB, ioA, 0.5, None,
                                                              ALU.is_ge),
                                    rd(r_io[0]))
                        r_io[0].note_write_partial(t4)
                        t5 = P.emit("vector",
                                    lambda e: e.tensor_tensor(ioA, ioA, ioB,
                                                              ALU.subtract),
                                    rd(r_io[0]))
                        r_io[0].note_write_partial(t5)
                        ts = P.emit("scalar",
                                    lambda e, n=n, dst=dst:
                                    e.activation(dst[:, n * 512:(n + 1) * 512], ioA,
                                                 AF.Sin, scale=2 * PI),
                                    rd(r_io[0]) + wr(rdst))
                        r_io[0].note_read(ts)
                        rdst.note_write_partial(ts)
                tvc = P.emit("vector",
                             lambda e, d=d:
                             e.tensor_tensor(sD[2][:, :], qb[d][:, :], sD[1][:, :], ALU.mult),
                             rd(r_qb[d], r_sD[1]) + wr(r_sD[2]))
                r_qb[d].note_read(tvc)
                r_sD[1].note_read(tvc)
                r_sD[2].note_write(tvc)
                tsc = P.emit("vector",
                             lambda e, d=d:
                             e.tensor_tensor_scan(sD[3][:, :], sD[2][:, :], sD[2][:, :],
                                                  carry[:, d:d + 1], ALU.add, ALU.bypass),
                             rd(r_sD[2], r_carry) + wr(r_sD[3]))
                r_sD[2].note_read(tsc)
                r_carry.note_read(tsc)
                r_sD[3].note_write(tsc)
                tcy = P.emit("vector",
                             lambda e, d=d: e.tensor_copy(carry[:, d:d + 1], sD[3][:, 1023:1024]),
                             rd(r_sD[3]) + wr(r_carry))
                r_sD[3].note_read(tcy)
                r_carry.note_write_partial(tcy)
                tr1 = P.emit("vector",
                             lambda e: e.tensor_tensor(sD[1][:, :], sD[1][:, :], sD[3][:, :],
                                                       ALU.mult),
                             rd(r_sD[1], r_sD[3]))
                r_sD[3].note_read(tr1)
                r_sD[1].note_write(tr1)
                tvs = P.emit("vector",
                             lambda e, d=d:
                             e.tensor_tensor(sD[2][:, :], qb[d][:, :], sD[0][:, :], ALU.mult),
                             rd(r_qb[d], r_sD[0]) + wr(r_sD[2]))
                r_qb[d].note_read(tvs)
                r_sD[0].note_read(tvs)
                r_sD[2].note_write(tvs)
                tsb = P.emit("vector",
                             lambda e, d=d:
                             e.tensor_tensor_scan(sD[3][:, :], sD[2][:, :], sD[2][:, :],
                                                  carry[:, 8 + d:9 + d], ALU.add, ALU.bypass),
                             rd(r_sD[2], r_carry) + wr(r_sD[3]))
                r_sD[2].note_read(tsb)
                r_carry.note_read(tsb)
                r_sD[3].note_write(tsb)
                tcy2 = P.emit("vector",
                              lambda e, d=d:
                              e.tensor_copy(carry[:, 8 + d:9 + d], sD[3][:, 1023:1024]),
                              rd(r_sD[3]) + wr(r_carry))
                r_sD[3].note_read(tcy2)
                r_carry.note_write_partial(tcy2)
                tr2 = P.emit("vector",
                             lambda e: e.tensor_tensor(sD[0][:, :], sD[0][:, :], sD[3][:, :],
                                                       ALU.mult),
                             rd(r_sD[0], r_sD[3]))
                r_sD[3].note_read(tr2)
                r_sD[0].note_write(tr2)
                tr3 = P.emit("vector",
                             lambda e, d=d:
                             e.tensor_tensor(kb[d][:, 0:1024], sD[1][:, :], sD[0][:, :],
                                             ALU.add),
                             rd(r_sD[1], r_sD[0]) + wr(r_kb[d]))
                r_sD[1].note_read(tr3)
                r_sD[0].note_read(tr3)
                r_kb[d].note_write(tr3)
            transpose_weight_to_wv(w_aWo)
            for m in range(NDT):
                for n in range(2):
                    bank = (m * 2 + n) % 4
                    for k in range(NDT):
                        wl = rd(r_A[3 + k], r_kb[k]) + CST
                        if k == 0:
                            wl += wr(r_pb[bank])
                        mm = P.emit("tensor",
                                    lambda e, k=k, m=m, n=n, bank=bank,
                                           s=(k == 0), p=(k == NDT - 1):
                                    e.matmul(pb[bank][:, :], A[3 + k][:, m * 128:(m + 1) * 128],
                                             kb[k][:, n * 512:(n + 1) * 512],
                                             start=s, stop=p),
                                    wl)
                        r_A[3 + k].note_read(mm)
                        r_kb[k].note_read(mm)
                        r_pb[bank].note_write_partial(mm)
                    ev = P.emit("vector",
                                lambda e, m=m, n=n, h=h, bank=bank:
                                e.tensor_tensor(
                                    xres[m][:, h * HT + n * 512: h * HT + n * 512 + 512],
                                    xres[m][:, h * HT + n * 512: h * HT + n * 512 + 512],
                                    pb[bank][:, :], ALU.add),
                                rd(r_pb[bank]) + wr(r_xres[m]))
                    r_pb[bank].note_read(ev)
                    r_xres[m].note_write_partial(ev)

    def save_x2_ln2():
        ticks = []
        for d in range(NDT):
            tk = P.dma("sync", x2_dram[d], xres[d][:, :], ds_x2, rd(r_xres[d]))
            r_xres[d].note_read(tk)
            r_x2d.note_write_partial(tk)
            ticks.append(tk)
        mx = max(t.dma_value for t in ticks)
        for t in ticks:
            t.dma_value = mx
        for h in range(2):
            role_copy(h, shifted=False)
            banks = ln_stats([0] * NDT)
            ln_apply_xres(h, banks)

    def finale():
        ticks = []
        for d in range(NDT):
            tk = P.dma("gpsimd", xres[d][:, :], x2_dram[d], ds_x2r,
                       rd(r_x2d) + wr(r_xres[d]), accum=ALU.add)
            r_xres[d].note_write(tk)
            ticks.append(tk)
        mx = max(t.dma_value for t in ticks)
        for t in ticks:
            t.dma_value = mx
        for tau in range(16):
            st = tau % 2
            rs = [r_A[2 * st], r_A[2 * st + 1]]
            evs = []
            for d in range(NDT):
                bank = (tau * NDT + d) % 4
                mm = P.emit("tensor",
                            lambda e, d=d, tau=tau, bank=bank:
                            e.transpose(pb[bank][:, 0:128],
                                        xres[d][:, tau * 128:(tau + 1) * 128],
                                        csts["identf"][:, :]),
                            rd(r_xres[d]) + wr(r_pb[bank]) + CST)
                r_xres[d].note_read(mm)
                r_pb[bank].note_write(mm)
                wl = rd(r_pb[bank]) + (wr(*rs) if d == 0 else [])
                ev = P.emit("scalar",
                            lambda e, d=d, st=st, bank=bank:
                            e.copy(stg[st][:, d * 128:(d + 1) * 128], pb[bank][:, 0:128]),
                            wl)
                r_pb[bank].note_read(ev)
                for r in rs:
                    r.note_write_partial(ev)
                evs.append(ev)
            tk = P.dma("sync", y_out[tau * 128:(tau + 1) * 128, :], stg[st],
                       ds_ys[st], evs)
            for r in rs:
                r.note_read(tk)

    load_input()
    outer_attention()
    tap("x2pre")
    save_x2_ln2()
    tap("ln2")
    for i in range(DEPTH):
        for h in range(2):
            if i == 0 and h == 0:
                role_copy(h, shifted=True)
            qkv_qk(i, h)
            qkv_v(i, h)
            qk_prep(i, h, 0, 0)
            qk_prep(i, h, 1, 0)
            qk_prep(i, h, 0, 1)
            qk_prep(i, h, 1, 1)
            attention(i, h, [h * 4, h * 4 + 1])
            attention(i, h, [h * 4 + 2, h * 4 + 3])
            wo_phase(i, h)
            if h == 0:
                for m in range(NDT):
                    tkh = P.emit("vector",
                                 lambda e, m=m:
                                 e.tensor_copy(kb[m][:, 0:256], kb[m][:, 1024:1280]),
                                 wr(r_kb[m]))
                    r_kb[m].note_write_partial(tkh)
            tap(f"attn{i}h{h}")
            role_copy(h, shifted=True)
            banks = ln_stats(SHIFT_OFFS)
            ln_apply_role(SHIFT_OFFS, banks, layer=i)
            if h == 0:
                nxt = (1, True)
            elif i < DEPTH - 1:
                nxt = (0, True)
            else:
                nxt = None
            ffn_phase(i, h, next_role=nxt)
            tap(f"layer{i}h{h}")
    finale()

    with nc.Block() as block:
        P.realize(block, eng_sems, dma_handles)

        @block.sync
        def _(sync):
            sync.wait_ge(dma_handles[ds_ys[0].key], 16 * ds_ys[0].count)
            sync.wait_ge(dma_handles[ds_ys[1].key], 16 * ds_ys[1].count)

    stack.close()
    return nc


def kernel(L0_attn_x, L0_attn_Wk, L0_attn_Wv, L0_attn_Wo, L0_attn_freqs,
           L0_ffn_Wqkv, L0_ffn_q_scale, L0_ffn_k_scale, L0_ffn_Wo,
           L0_ffn_ln_g, L0_ffn_ln_b, L0_ffn_W1, L0_ffn_b1, L0_ffn_W2,
           L0_ffn_b2):
    global LAST
    inp = dict(L0_ffn_ln_g=L0_ffn_ln_g, L0_ffn_ln_b=L0_ffn_ln_b,
               L0_ffn_b2=L0_ffn_b2, L0_ffn_q_scale=L0_ffn_q_scale,
               L0_ffn_k_scale=L0_ffn_k_scale, L0_ffn_b1=L0_ffn_b1,
               L0_attn_freqs=L0_attn_freqs)
    C = host_constants()
    V = pack_vectors(inp)
    x = np.ascontiguousarray(np.asarray(L0_attn_x, np.float32))

    nc = build_program()

    base = {
        "aWv": np.ascontiguousarray(np.asarray(L0_attn_Wv, np.float32)),
        "aWo": np.ascontiguousarray(np.asarray(L0_attn_Wo, np.float32)),
        "Wqkv": np.ascontiguousarray(np.asarray(L0_ffn_Wqkv, np.float32)),
        "Wo": np.ascontiguousarray(np.asarray(L0_ffn_Wo, np.float32)),
        "W1": np.ascontiguousarray(np.asarray(L0_ffn_W1, np.float32)),
        "W2": np.ascontiguousarray(np.asarray(L0_ffn_W2, np.float32)),
    }
    for nm, arr in C.items():
        base["c_" + nm] = np.ascontiguousarray(arr)
    for nm, arr in V.items():
        base["c_" + nm] = np.ascontiguousarray(arr)

    in_maps = []
    for c in range(NCORES):
        m = dict(base)
        m["x"] = np.ascontiguousarray(x[c])
        in_maps.append(m)

    res = run_bass_kernel_spmd(nc, in_maps, core_ids=list(range(NCORES)))
    LAST = res
    out = np.stack([np.asarray(r["y"], np.float32) for r in res.results], axis=0)
    return out


# version 18
# speedup vs baseline: 1.0633x; 1.0633x over previous
import numpy as np
import ml_dtypes
import concourse.bass as bass
import concourse.mybir as mybir
from concourse.bass_utils import run_bass_kernel_spmd

F32 = mybir.dt.float32
BF16 = mybir.dt.bfloat16
AF = mybir.ActivationFunctionType
ALU = mybir.AluOpType
AX = mybir.AxisListType
BF = ml_dtypes.bfloat16

NCORES = 8
D = 1024
T = 2048
HT = 1024
NDT = 8
DEPTH = 6
NH = 16
DH = 64
WIN = 256
DI = 2730
PI = float(np.pi)
QK_SCALE = 8.0
NEG = -1.0e30

FFN_CHUNKS = [(k * 128, 128) for k in range(21)] + [(2688, 42)]
NCH = len(FFN_CHUNKS)

DEBUG_TAPS = []
LAST = None


def host_constants():
    C = {}
    C["identb"] = np.eye(128, dtype=np.float32).astype(BF)
    C["identf"] = np.eye(128, dtype=np.float32)
    P = np.zeros((128, 128), np.float32)
    for b in (0, 64):
        for d in range(64):
            P[b + (d + 32) % 64, b + d] = 1.0
    C["pp"] = P.astype(BF)
    i = np.arange(128)[:, None]
    jj = np.arange(512)[None, :]
    valid = jj < 384
    m_main = np.where((jj >= i) & (jj <= i + 256) & valid, 0.0, NEG)
    m_w0q0 = np.where((jj >= 256) & (jj <= 256 + i) & valid, 0.0, NEG)
    m_w0q1 = np.where((jj >= 128) & (jj <= 256 + i) & valid, 0.0, NEG)
    C["masks"] = np.concatenate([m_main, m_w0q0, m_w0q1], 1).astype(BF)
    C["onesc"] = np.ones((128, 1), np.float32).astype(BF)
    C["onesf"] = np.ones((1, 128), np.float32)
    C["epscol"] = np.full((128, 1), 1e-5, np.float32)
    C["negpicol"] = np.full((128, 1), -PI, np.float32)
    hs = np.zeros((128, 8 * 16), np.float32)
    hb = np.zeros((16, 8 * 128), np.float32)
    for m in range(8):
        for p in range(128):
            c = 2 * m + p // 64
            hs[p, m * 16 + c] = 1.0
            hb[c, m * 128 + p] = 1.0
    C["hsum"] = hs.astype(BF)
    C["hbm"] = hb.astype(BF)
    half = np.arange(0, DH, 2, dtype=np.float64)
    inv_freq = 1.0 / 10000.0 ** (half / DH)
    t = np.arange(T, dtype=np.float64)
    fr = t[:, None] * inv_freq[None, :]
    freqs = np.concatenate([fr, fr], -1)
    base_scale = (half + 0.4 * DH) / (1.4 * DH)
    power = (t - T // 2) / (WIN // 2)
    scale = base_scale[None, :] ** power[:, None]
    scale = np.concatenate([scale, scale], -1)
    sign = np.where(np.arange(DH) < 32, 1.0, -1.0)[None, :]
    csk = np.cos(freqs) / scale
    ssk = np.sin(freqs) / scale * sign
    scl2 = QK_SCALE * scale * scale
    for nm, tab in (("csk", csk), ("ssk", ssk), ("scl2", scl2)):
        C[nm] = np.tile(tab.T, (2, 1)).astype(np.float32).astype(BF)
    return C


def pack_vectors(inp):
    V = {}
    g = np.asarray(inp["L0_ffn_ln_g"], np.float32)
    b = np.asarray(inp["L0_ffn_ln_b"], np.float32)
    b2 = np.asarray(inp["L0_ffn_b2"], np.float32)
    for nm, a in (("gcol", g), ("bcol", b), ("b2col", b2)):
        out = np.zeros((128, DEPTH * 8), np.float32)
        for i in range(DEPTH):
            for m in range(8):
                out[:, i * 8 + m] = a[i, m * 128:(m + 1) * 128]
        V[nm] = out
    qs = np.asarray(inp["L0_ffn_q_scale"], np.float32)
    ks = np.asarray(inp["L0_ffn_k_scale"], np.float32)
    V["qscol"] = np.concatenate([qs.T, qs.T], 0)
    V["kscol"] = np.concatenate([ks.T, ks.T], 0)
    b1 = np.asarray(inp["L0_ffn_b1"], np.float32)
    b1a = np.zeros((128, DEPTH * NCH), np.float32)
    b1g = np.zeros((128, DEPTH * NCH), np.float32)
    for i in range(DEPTH):
        for c, (off, sz) in enumerate(FFN_CHUNKS):
            b1a[:sz, i * NCH + c] = b1[i, off:off + sz]
            b1g[:sz, i * NCH + c] = b1[i, DI + off:DI + off + sz]
    V["b1acol"] = b1a
    V["b1gcol"] = b1g
    rows = np.zeros((128, 1024), np.float32)
    rows[0, :] = np.asarray(inp["L0_attn_freqs"], np.float32)
    rows[32, :] = np.arange(0, HT, dtype=np.float32)
    rows[64, :] = np.arange(HT, T, dtype=np.float32)
    V["rows2init"] = rows
    return V


ENGINES = ("tensor", "vector", "scalar", "sync", "gpsimd")


class Tick:
    __slots__ = ("eng", "seq", "needed", "value", "dma_sem", "dma_value")

    def __init__(self, eng, seq):
        self.eng = eng
        self.seq = seq
        self.needed = False
        self.value = None
        self.dma_sem = None
        self.dma_value = None


class Op:
    __slots__ = ("fn", "tick", "waits", "do_inc")

    def __init__(self, fn, tick, waits):
        self.fn = fn
        self.tick = tick
        self.waits = waits
        self.do_inc = False


class DmaSem:
    __slots__ = ("key", "count")

    def __init__(self, key):
        self.key = key
        self.count = 0


class Prog:
    def __init__(self):
        self.streams = {e: [] for e in ENGINES}
        self.n_dma_sems = 0

    def new_dma_sem(self):
        s = DmaSem(("dma", self.n_dma_sems))
        self.n_dma_sems += 1
        return s

    def _filter(self, waits):
        best_eng = {}
        best_dma = {}
        for tk in waits:
            if tk is None:
                continue
            if tk.dma_sem is not None:
                v = best_dma.get(tk.dma_sem)
                if v is None or tk.dma_value > v:
                    best_dma[tk.dma_sem] = tk.dma_value
            else:
                v = best_eng.get(tk.eng)
                if v is None or tk.seq > v.seq:
                    best_eng[tk.eng] = tk
        out = []
        for eng, tk in best_eng.items():
            tk.needed = True
            out.append(("eng", eng, tk))
        for key, val in best_dma.items():
            out.append(("dma", key, val))
        return out

    def emit(self, eng, fn, waits=()):
        tick = Tick(eng, len(self.streams[eng]))
        self.streams[eng].append(Op(fn, tick, self._filter(waits)))
        return tick

    def dma(self, queue, out_ap, in_ap, dsem, waits=(), accum=None):
        tick = Tick(queue, len(self.streams[queue]))
        dsem.count += 1
        tick.dma_sem = dsem.key
        tick.dma_value = 16 * dsem.count
        if accum is None:
            fn = lambda e, o=out_ap, i=in_ap: e.dma_start(o, i)
        else:
            fn = lambda e, o=out_ap, i=in_ap, a=accum: e.dma_start(o, i, accum_op=a)
        op = Op(fn, tick, self._filter(waits))
        op.do_inc = True
        self.streams[queue].append(op)
        return tick

    def realize(self, block, eng_sems, dma_sems):
        for ename in ENGINES:
            c = 0
            for op in self.streams[ename]:
                if op.tick.dma_sem is None and op.tick.needed:
                    c += 1
                    op.tick.value = c
                    op.do_inc = True

        def make_body(ename):
            ops = self.streams[ename]
            sem_self = eng_sems.get(ename)

            def body(eng):
                seen = {}
                for op in ops:
                    for kind, key, val in op.waits:
                        if kind == "eng":
                            v = val.value
                            sem = eng_sems[key]
                        else:
                            v = val
                            sem = dma_sems[key]
                        if seen.get(key, -1) >= v:
                            continue
                        seen[key] = v
                        eng.wait_ge(sem, v)
                    inst = op.fn(eng)
                    if op.do_inc:
                        if op.tick.dma_sem is not None:
                            inst.then_inc(dma_sems[op.tick.dma_sem], 16)
                        else:
                            inst.then_inc(sem_self, 1)
            return body

        for ename in ENGINES:
            getattr(block, ename)(make_body(ename))


class Res:
    __slots__ = ("writers", "readers")

    def __init__(self):
        self.writers = []
        self.readers = []

    def rdeps(self):
        return list(self.writers)

    def wdeps(self):
        return self.readers + self.writers

    def note_read(self, tick):
        self.readers.append(tick)

    def note_write(self, tick):
        self.writers = [tick]
        self.readers = []

    def note_write_partial(self, tick):
        self.writers.append(tick)
        if len(self.writers) > 24:
            self.writers = self.writers[-24:]


def rd(*rs):
    out = []
    for r in rs:
        out += r.rdeps()
    return out


def wr(*rs):
    out = []
    for r in rs:
        out += r.wdeps()
    return out


def build_program():
    import contextlib
    nc = bass.Bass("TRN2", target_bir_lowering=False, debug=False)
    P = Prog()
    stack = contextlib.ExitStack()
    sbuf = lambda nm, shp, dt: stack.enter_context(nc.sbuf_tensor(nm, shp, dt))
    psum = lambda nm, shp, dt: stack.enter_context(nc.psum_tensor(nm, shp, dt))
    sem = lambda nm: stack.enter_context(nc.semaphore(nm))

    x_in = nc.dram_tensor("x", [T, D], F32, kind="ExternalInput").ap()
    y_out = nc.dram_tensor("y", [T, D], F32, kind="ExternalOutput").ap()
    x2_dram = nc.dram_tensor("x2scr", [NDT, 128, T], F32, kind="Internal").ap()
    w_aWv = nc.dram_tensor("aWv", [D, D], F32, kind="ExternalInput").ap()
    w_aWo = nc.dram_tensor("aWo", [D, D], F32, kind="ExternalInput").ap()
    w_qkv = nc.dram_tensor("Wqkv", [DEPTH, D, 3 * D], F32, kind="ExternalInput").ap()
    w_wo = nc.dram_tensor("Wo", [DEPTH, D, D], F32, kind="ExternalInput").ap()
    w_w1 = nc.dram_tensor("W1", [DEPTH, D, 2 * DI], F32, kind="ExternalInput").ap()
    w_w2 = nc.dram_tensor("W2", [DEPTH, DI, D], F32, kind="ExternalInput").ap()
    dbg_drams = {}
    for nm in DEBUG_TAPS:
        dbg_drams[nm] = nc.dram_tensor("dbg_" + nm, [NDT, 128, T], F32,
                                       kind="ExternalOutput").ap()

    cst_specs = {
        "identb": ([128, 128], BF16), "identf": ([128, 128], F32),
        "pp": ([128, 128], BF16), "masks": ([128, 1536], BF16),
        "onesc": ([128, 1], BF16), "onesf": ([1, 128], F32), "epscol": ([128, 1], F32), "negpicol": ([128, 1], F32), "hsum": ([128, 128], BF16),
        "hbm": ([16, 1024], BF16),
        "csk": ([128, T], BF16), "ssk": ([128, T], BF16), "scl2": ([128, T], BF16),
        "gcol": ([128, DEPTH * 8], F32), "bcol": ([128, DEPTH * 8], F32),
        "b2col": ([128, DEPTH * 8], F32),
        "qscol": ([128, DEPTH], F32), "kscol": ([128, DEPTH], F32),
        "b1acol": ([128, DEPTH * NCH], F32), "b1gcol": ([128, DEPTH * NCH], F32),
        "rows2init": ([128, 1024], F32),
    }
    cst_dram = {nm: nc.dram_tensor("c_" + nm, list(shp), dt, kind="ExternalInput").ap()
                for nm, (shp, dt) in cst_specs.items()}

    xres = [sbuf(f"xres{i}", [128, T], F32) for i in range(NDT)]
    role = [sbuf(f"role{i}", [128, 1028], BF16) for i in range(NDT)]
    qb = [sbuf(f"qb{i}", [128, HT], BF16) for i in range(NDT)]
    kb = [sbuf(f"kb{i}", [128, HT + 256], BF16) for i in range(NDT)]
    vb = [sbuf(f"vb{i}", [128, 1024], BF16) for i in range(10)]
    arena = sbuf("arena", [128, 11 * 1024], BF16)
    A = [arena[:, i * 1024:(i + 1) * 1024] for i in range(11)]
    stg = [arena[:, 0:2048].bitcast(F32), arena[:, 2048:4096].bitcast(F32)]
    w2v = [arena[:, 8448:11264], arena[:, 5632:8448]]
    W2_SLOTMAP = [(8, 9, 10), (5, 6, 7, 8)]
    probsB = sbuf("probsB", [128, 2048], BF16)
    probs = [probsB[:, 0:1024], probsB[:, 1024:2048]]
    probsT = sbuf("probsT", [128, 1024], BF16)
    sD = [sbuf(f"sD{i}", [128, 1024], BF16) for i in range(3)]
    sD.append(vb[0])
    iostage = [sbuf("iost0", [128, 1024], F32)]
    iostage = [iostage[0], iostage[0]]
    rowscr = sbuf("rowscr", [1, 1024], F32)
    rqrowt = sbuf("rqrowt", [16, 512], F32)
    rqcolp = sbuf("rqcolp", [128, 128], F32)
    rqkb_hi = sbuf("rqkbh", [16, 512], BF16)
    rqkb_lo = sbuf("rqkbl", [16, 512], BF16)
    negmax2 = sbuf("ngm2", [128, 2], F32)
    zcol2 = sbuf("zc2", [128, 2], F32)
    carry = sbuf("carry", [128, 16], F32)
    csts = {nm: sbuf("s_" + nm, list(shp), dt) for nm, (shp, dt) in cst_specs.items()}

    psS = [psum("psS0", [128, 1024], F32), psum("psS1", [128, 1024], F32)]
    pb = [psS[0][:, 0:512], psS[0][:, 512:1024], psS[1][:, 0:512], psS[1][:, 512:1024]]
    pb += [psum(f"pb{i}", [128, 512], F32)[:, :] for i in (4, 5, 6)]
    pT = psum("pT", [128, 1024], BF16)

    eng_sems = {e: sem("sem_" + e) for e in ("tensor", "vector", "scalar")}
    dma_handles = {}

    def mk_dsem():
        s = P.new_dma_sem()
        dma_handles[s.key] = sem(f"dq{s.key[1]}")
        return s

    ds_cst = mk_dsem()
    ds_x = mk_dsem()
    ds_slot = [mk_dsem() for _ in range(11)]
    ds_w2 = [mk_dsem() for _ in range(2)]
    ds_x2 = mk_dsem()
    ds_x2r = mk_dsem()
    ds_out = mk_dsem()
    ds_dbg = mk_dsem()
    ds_xs = [mk_dsem(), mk_dsem()]
    ds_ys = [mk_dsem(), mk_dsem()]

    r_xres = [Res() for _ in range(NDT)]
    r_role = [Res() for _ in range(NDT)]
    r_qb = [Res() for _ in range(NDT)]
    r_kb = [Res() for _ in range(NDT)]
    r_vb = [Res() for _ in range(10)]
    r_A = [Res() for _ in range(11)]
    r_probs4 = [[Res(), Res()], [Res(), Res()]]
    r_probsT = Res()
    r_sD = [Res() for _ in range(3)]
    r_sD.append(r_vb[0])
    _rio = Res()
    r_io = [_rio, _rio]
    r_rowscr = Res()
    r_rq = Res()
    r_rqcolp = Res()
    r_rqkh = Res()
    r_rqkl = Res()
    r_negmax2 = Res()
    r_zcol2 = Res()
    r_carry = Res()
    r_pb = [Res() for _ in range(7)]
    r_pTh = [Res(), Res()]
    r_x2d = Res()

    cst_last = None
    for nm in cst_specs:
        cst_last = P.dma("sync", csts[nm][:, :], cst_dram[nm], ds_cst)
    CST = [cst_last]

    rows2 = csts["rows2init"]
    ones1 = csts["onesf"][0:1, :]
    trow = None
    freqsrow = rows2[0:1, :]
    rowMean = rowscr[0:1, 0:512]
    rowRstd = rowscr[0:1, 512:1024]
    rqrow = rqrowt[0:16, :]

    def mask_ap(which, n):
        return csts["masks"][0:128, which * 512:which * 512 + n]

    tz = P.emit("vector", lambda e: e.memset(carry[:, :], 0.0), CST)
    r_carry.note_write(tz)
    tzp = P.emit("vector", lambda e: e.memset(pT[:, :].bitcast(F32), 0.0), CST)
    r_pTh[0].note_write(tzp)
    r_pTh[1].note_write(tzp)
    for i2 in range(NDT):
        tk = P.emit("vector", lambda e, i2=i2: e.memset(kb[i2][:, 0:256], 0.0), CST)
        r_kb[i2].note_write(tk)

    def load_input():
        for tau in range(16):
            st = tau % 2
            rs = [r_A[2 * st], r_A[2 * st + 1]]
            dtick = P.dma("sync", stg[st], x_in[tau * 128:(tau + 1) * 128, :],
                          ds_xs[st], wr(*rs))
            for r in rs:
                r.note_write(dtick)
            for d in range(NDT):
                bank = (tau * NDT + d) % 4
                mm = P.emit("tensor",
                            lambda e, st=st, d=d, bank=bank:
                            e.transpose(pb[bank][:, 0:128],
                                        stg[st][:, d * 128:(d + 1) * 128],
                                        csts["identf"][:, :]),
                            [dtick] + wr(r_pb[bank]) + CST)
                r_pb[bank].note_write(mm)
                for r in rs:
                    r.note_read(mm)
                ev = P.emit("scalar",
                            lambda e, d=d, tau=tau, bank=bank:
                            e.copy(xres[d][:, tau * 128:(tau + 1) * 128], pb[bank][:, 0:128]),
                            rd(r_pb[bank]) + wr(r_xres[d]))
                r_pb[bank].note_read(ev)
                r_xres[d].note_write_partial(ev)

    def tap(stage):
        if stage not in dbg_drams:
            return
        dr = dbg_drams[stage]
        for d in range(NDT):
            tk = P.dma("sync", dr[d], xres[d][:, :], ds_dbg, rd(r_xres[d]))
            r_xres[d].note_read(tk)

    def role_copy_tile(h, shifted, k):
        role_copy_some(h, shifted, [k])

    def role_copy(h, shifted):
        role_copy_some(h, shifted, list(range(NDT)))

    def role_copy_some(h, shifted, ks):
        for k in ks:
            if shifted:
                if h == 0:
                    tz = P.emit("vector", lambda e, k=k: e.memset(role[k][:, 0:1], 0.0),
                                wr(r_role[k]))
                    r_role[k].note_write(tz)
                    tc = P.emit("vector",
                                lambda e, k=k: e.tensor_copy(role[k][:, 1:1025], xres[k][:, 0:1024]),
                                rd(r_xres[k]) + wr(r_role[k]))
                    r_role[k].note_write_partial(tc)
                else:
                    tc = P.emit("vector",
                                lambda e, k=k, h=h:
                                e.tensor_copy(role[k][:, 0:1025], xres[k][:, h * HT - 1:h * HT + 1024]),
                                rd(r_xres[k]) + wr(r_role[k]))
                    r_role[k].note_write(tc)
            else:
                tc = P.emit("vector",
                            lambda e, k=k, h=h:
                            e.tensor_copy(role[k][:, 0:1024], xres[k][:, h * HT:(h + 1) * HT]),
                            rd(r_xres[k]) + wr(r_role[k]))
                r_role[k].note_write(tc)
            r_xres[k].note_read(tc)

    def ln_stats(offs):
        for cch, bank in ((0, 4), (1, 5)):
            for k in range(NDT):
                wl = rd(r_role[k]) + CST
                if k == 0:
                    wl += wr(r_pb[bank])
                mm = P.emit("tensor",
                            lambda e, k=k, cch=cch, bank=bank, o=offs[k]:
                            e.matmul(pb[bank][0:1, :], csts["onesc"][:, 0:1],
                                     role[k][:, o + cch * 512: o + cch * 512 + 512],
                                     start=(k == 0), stop=(k == NDT - 1)),
                            wl)
                r_role[k].note_read(mm)
                r_pb[bank].note_write_partial(mm)
        for cch, bank in ((0, 6), (1, 3)):
            for k in range(NDT):
                sl = k % 2
                sq = P.emit("vector",
                            lambda e, k=k, cch=cch, sl=sl, o=offs[k]:
                            e.tensor_tensor(sD[sl][:, 0:512],
                                            role[k][:, o + cch * 512: o + cch * 512 + 512],
                                            role[k][:, o + cch * 512: o + cch * 512 + 512],
                                            ALU.mult),
                            rd(r_role[k]) + wr(r_sD[sl]))
                r_sD[sl].note_write(sq)
                r_role[k].note_read(sq)
                wl = [sq] + CST
                if k == 0:
                    wl += wr(r_pb[bank])
                mm = P.emit("tensor",
                            lambda e, k=k, bank=bank, sl=sl:
                            e.matmul(pb[bank][0:1, :], csts["onesc"][:, 0:1],
                                     sD[sl][:, 0:512], start=(k == 0), stop=(k == NDT - 1)),
                            wl)
                r_pb[bank].note_write_partial(mm)
                r_sD[sl].note_read(mm)
        banks = {0: (0, 2), 1: (1, 6)}
        for cch, (sbank, qbank) in ((0, (4, 6)), (1, (5, 3))):
            cc = cch * 512
            rbank, mbank = banks[cch]
            t1 = P.emit("vector",
                        lambda e, sbank=sbank:
                        e.tensor_scalar(rowMean[:, :], pb[sbank][0:1, :],
                                        1.0 / D, None, ALU.mult),
                        rd(r_pb[sbank]) + wr(r_rowscr))
            r_pb[sbank].note_read(t1)
            r_rowscr.note_write_partial(t1)
            tm = P.emit("tensor",
                        lambda e, mbank=mbank:
                        e.matmul(pb[mbank][:, :], ones1, rowMean[:, :],
                                 start=True, stop=True),
                        rd(r_rowscr) + wr(r_pb[mbank]) + CST)
            r_rowscr.note_read(tm)
            r_pb[mbank].note_write(tm)
            t2 = P.emit("vector",
                        lambda e, qbank=qbank:
                        e.tensor_scalar(rowRstd[:, :], pb[qbank][0:1, :],
                                        1.0 / D, None, ALU.mult),
                        rd(r_pb[qbank]) + wr(r_rowscr))
            r_pb[qbank].note_read(t2)
            r_rowscr.note_write_partial(t2)
            t3 = P.emit("vector",
                        lambda e:
                        e.tensor_tensor(rowMean[:, :], rowMean[:, :],
                                        rowMean[:, :], ALU.mult),
                        [tm] + rd(r_rowscr))
            r_rowscr.note_write_partial(t3)
            t4 = P.emit("vector",
                        lambda e:
                        e.tensor_tensor(rowRstd[:, :], rowRstd[:, :],
                                        rowMean[:, :], ALU.subtract),
                        rd(r_rowscr))
            r_rowscr.note_write_partial(t4)
            t5 = P.emit("scalar",
                        lambda e:
                        e.activation(rowRstd[:, :], rowRstd[:, :],
                                     AF.Sqrt, bias=csts["epscol"][0:1, 0:1]),
                        rd(r_rowscr))
            r_rowscr.note_write_partial(t5)
            t6 = P.emit("vector",
                        lambda e:
                        e.reciprocal(rowRstd[:, :], rowRstd[:, :]),
                        rd(r_rowscr))
            r_rowscr.note_write_partial(t6)
            tb = P.emit("tensor",
                        lambda e, rbank=rbank:
                        e.matmul(pb[rbank][:, :], ones1, rowRstd[:, :],
                                 start=True, stop=True),
                        rd(r_rowscr) + wr(r_pb[rbank]) + CST)
            r_rowscr.note_read(tb)
            r_pb[rbank].note_write(tb)
        return banks

    def ln_apply_role(offs, banks, layer):
        for m in range(NDT):
            for cch in range(2):
                rbank, mbank = banks[cch]
                cc = offs[m] + cch * 512
                sl = m % 2
                t1 = P.emit("vector",
                            lambda e, m=m, cc=cc, mbank=mbank, sl=sl:
                            e.tensor_tensor(sD[sl][:, 0:512], role[m][:, cc:cc + 512],
                                            pb[mbank][:, :], ALU.subtract),
                            rd(r_role[m], r_pb[mbank]) + wr(r_sD[sl]))
                r_role[m].note_read(t1)
                r_pb[mbank].note_read(t1)
                r_sD[sl].note_write(t1)
                if layer is None:
                    t2 = P.emit("vector",
                                lambda e, m=m, cc=cc, rbank=rbank, sl=sl:
                                e.tensor_tensor(role[m][:, cc:cc + 512], sD[sl][:, 0:512],
                                                pb[rbank][:, :], ALU.mult),
                                rd(r_sD[sl], r_pb[rbank]) + wr(r_role[m]))
                    r_sD[sl].note_read(t2)
                    r_pb[rbank].note_read(t2)
                    r_role[m].note_write_partial(t2)
                else:
                    t2 = P.emit("vector",
                                lambda e, cc=cc, rbank=rbank, sl=sl:
                                e.tensor_tensor(sD[sl][:, 0:512], sD[sl][:, 0:512],
                                                pb[rbank][:, :], ALU.mult),
                                rd(r_sD[sl], r_pb[rbank]))
                    r_pb[rbank].note_read(t2)
                    r_sD[sl].note_write(t2)
                    gc = layer * 8 + m
                    t3 = P.emit("vector",
                                lambda e, m=m, cc=cc, gc=gc, sl=sl:
                                e.tensor_scalar(role[m][:, cc:cc + 512], sD[sl][:, 0:512],
                                                csts["gcol"][:, gc:gc + 1],
                                                csts["bcol"][:, gc:gc + 1],
                                                ALU.mult, ALU.add),
                                rd(r_sD[sl]) + wr(r_role[m]) + CST)
                    r_sD[sl].note_read(t3)
                    r_role[m].note_write_partial(t3)

    def ln_apply_xres(h, banks):
        for m in range(NDT):
            for cch in range(2):
                rbank, mbank = banks[cch]
                cc = h * HT + cch * 512
                t1 = P.emit("vector",
                            lambda e, m=m, cc=cc, mbank=mbank:
                            e.tensor_tensor(xres[m][:, cc:cc + 512], xres[m][:, cc:cc + 512],
                                            pb[mbank][:, :], ALU.subtract),
                            rd(r_pb[mbank]) + wr(r_xres[m]))
                r_pb[mbank].note_read(t1)
                r_xres[m].note_write_partial(t1)
                t2 = P.emit("vector",
                            lambda e, m=m, cc=cc, rbank=rbank:
                            e.tensor_tensor(xres[m][:, cc:cc + 512], xres[m][:, cc:cc + 512],
                                            pb[rbank][:, :], ALU.mult),
                            rd(r_pb[rbank]) + wr(r_xres[m]))
                r_pb[rbank].note_read(t2)
                r_xres[m].note_write_partial(t2)

    def load_rot(src, slot, three_d=False):
        dst = A[slot].rearrange("p (k m) -> p k m", k=8) if three_d else A[slot][:, :]
        tk = P.dma("gpsimd", dst, src, ds_slot[slot], wr(r_A[slot]))
        r_A[slot].note_write(tk)
        return tk

    def load_w1(i, ch, gate, slot):
        off, sz = FFN_CHUNKS[ch]
        c0 = off + (DI if gate else 0)
        src = w_w1[i, :, c0:c0 + sz].rearrange("(kc p) m -> p kc m", p=128)
        dst = A[slot].rearrange("p (k m) -> p k m", k=8)[:, :, 0:sz]
        tk = P.dma("gpsimd", dst, src, ds_slot[slot], wr(r_A[slot]))
        r_A[slot].note_write(tk)
        return tk

    def load_w2(i, mo, j):
        slots = [r_A[s] for s in W2_SLOTMAP[j]]
        src = w_w2[i, 0:2688, mo * 128:(mo + 1) * 128].rearrange("(c p) m -> p c m", p=128)
        dstw2 = w2v[j][:, 0:2688].rearrange("p (c m) -> p c m", c=21)
        tk1 = P.dma("gpsimd", dstw2, src, ds_w2[j], wr(*slots))
        src2 = w_w2[i, 2688:2730, mo * 128:(mo + 1) * 128]
        tk2 = P.dma("gpsimd", w2v[j][0:42, 2688:2816], src2, ds_w2[j])
        for s in W2_SLOTMAP[j]:
            r_A[s].note_write(tk1)
            r_A[s].note_write_partial(tk2)
        return [tk1, tk2]

    SHIFT_OFFS = [1, 1, 1, 1, 0, 0, 0, 0]

    def qkv_qk(i, h):
        offs = SHIFT_OFFS
        wts = {}
        def ld(m):
            if m < 16 and m not in wts:
                wts[m] = load_rot(w_qkv[i, :, m * 128:(m + 1) * 128]
                                  .rearrange("(kc p) m -> p kc m", p=128), m % 3,
                                  three_d=True)
        ld(0); ld(1)
        for m in range(16):
            slot = m % 3
            ld(m + 1)
            wt = wts[m]
            dstbuf, dres = (qb, r_qb) if m < 8 else (kb, r_kb)
            mi = m % 8
            col0 = 0 if m < 8 else 256
            for k in range(NDT):
                for n in range(2):
                    bank = (m * 2 + n) % 4
                    wl = [wt] + rd(r_role[k]) + CST
                    if k == 0:
                        wl += wr(r_pb[bank])
                    mm = P.emit("tensor",
                                lambda e, slot=slot, k=k, n=n, bank=bank, o=offs[k],
                                       s=(k == 0), p=(k == NDT - 1):
                                e.matmul(pb[bank][:, :], A[slot][:, k * 128:(k + 1) * 128],
                                         role[k][:, o + n * 512: o + n * 512 + 512],
                                         start=s, stop=p),
                                wl)
                    r_A[slot].note_read(mm)
                    r_role[k].note_read(mm)
                    r_pb[bank].note_write_partial(mm)
            for n in range(2):
                bank = (m * 2 + n) % 4
                ev = P.emit("scalar",
                            lambda e, mi=mi, n=n, bank=bank, dstbuf=dstbuf, col0=col0:
                            e.copy(dstbuf[mi][:, col0 + n * 512: col0 + n * 512 + 512],
                                   pb[bank][:, :]),
                            rd(r_pb[bank]) + wr(dres[mi]))
                r_pb[bank].note_read(ev)
                dres[mi].note_write_partial(ev)
    def qkv_v(i, h):
        offs = SHIFT_OFFS
        vt = []
        for kc in range(NDT):
            slot = 3 + kc
            tk = P.dma("gpsimd", A[slot][:, :], w_qkv[i, kc * 128:(kc + 1) * 128, 2048:3072],
                       ds_slot[slot], wr(r_A[slot]))
            r_A[slot].note_write(tk)
            vt.append(tk)
        for tloc in range(8):
            tau = h * 8 + tloc
            vslot = tau % 10
            for n in range(2):
                bank = (tloc * 2 + n) % 4
                for k in range(NDT):
                    wl = [vt[k]] + rd(r_role[k]) + CST
                    if k == 0:
                        wl += wr(r_pb[bank])
                    mm = P.emit("tensor",
                                lambda e, k=k, n=n, bank=bank, tloc=tloc, o=offs[k],
                                       s=(k == 0), p=(k == NDT - 1):
                                e.matmul(pb[bank][:, :],
                                         role[k][:, o + tloc * 128: o + tloc * 128 + 128],
                                         A[3 + k][:, n * 512:(n + 1) * 512],
                                         start=s, stop=p),
                                wl)
                    r_A[3 + k].note_read(mm)
                    r_role[k].note_read(mm)
                    r_pb[bank].note_write_partial(mm)
                ev = P.emit("scalar",
                            lambda e, vslot=vslot, n=n, bank=bank:
                            e.copy(vb[vslot][:, n * 512:(n + 1) * 512], pb[bank][:, :]),
                            rd(r_pb[bank]) + wr(r_vb[vslot]))
                r_pb[bank].note_read(ev)
                r_vb[vslot].note_write_partial(ev)

    def qk_prep(i, h, which, cch):
        buf, rbuf = (qb, r_qb) if which == 0 else (kb, r_kb)
        col0 = 0 if which == 0 else 256
        sccol = csts["qscol"] if which == 0 else csts["kscol"]
        if True:
            bank = 4
            for m in range(NDT):
                sl = m % 2
                sq = P.emit("vector",
                            lambda e, m=m, cch=cch, sl=sl, buf=buf, col0=col0:
                            e.tensor_tensor(sD[sl][:, 0:512],
                                            buf[m][:, col0 + cch * 512: col0 + cch * 512 + 512],
                                            buf[m][:, col0 + cch * 512: col0 + cch * 512 + 512],
                                            ALU.mult),
                            rd(rbuf[m]) + wr(r_sD[sl]))
                r_sD[sl].note_write(sq)
                rbuf[m].note_read(sq)
                wl = [sq] + CST
                if m == 0:
                    wl += wr(r_pb[bank])
                mm = P.emit("tensor",
                            lambda e, m=m, bank=bank, sl=sl, s=(m == 0), p=(m == NDT - 1):
                            e.matmul(pb[bank][0:16, :], csts["hsum"][:, m * 16:(m + 1) * 16],
                                     sD[sl][:, 0:512], start=s, stop=p),
                            wl)
                r_pb[bank].note_write_partial(mm)
                r_sD[sl].note_read(mm)
            t1 = P.emit("scalar",
                        lambda e, bank=bank:
                        e.activation(rqrow[:, :], pb[bank][0:16, :], AF.Sqrt),
                        rd(r_pb[bank]) + wr(r_rq))
            r_pb[bank].note_read(t1)
            r_rq.note_write(t1)
            t2 = P.emit("vector",
                        lambda e:
                        e.tensor_scalar(rqrow[:, :], rqrow[:, :],
                                        1e-12, None, ALU.max),
                        rd(r_rq))
            r_rq.note_write_partial(t2)
            t3 = P.emit("vector",
                        lambda e:
                        e.reciprocal(rqrow[:, :], rqrow[:, :]),
                        rd(r_rq))
            r_rq.note_write_partial(t3)
        if which == 0:
            for tch in range(4):
                bankT = 6 if tch % 2 == 0 else 3
                tt = P.emit("tensor",
                            lambda e, tch=tch, bankT=bankT:
                            e.transpose(pb[bankT][:, 0:16], rqrow[0:16, tch * 128:(tch + 1) * 128],
                                        csts["identf"][0:16, 0:16]),
                            rd(r_rq) + wr(r_pb[bankT]) + CST)
                r_rq.note_read(tt)
                r_pb[bankT].note_write(tt)
                ev = P.emit("scalar",
                            lambda e, tch=tch, cch=cch, bankT=bankT:
                            e.copy(rqcolp[:, cch * 64 + tch * 16: cch * 64 + tch * 16 + 16],
                                   pb[bankT][:, 0:16]),
                            rd(r_pb[bankT]) + wr(r_rqcolp))
                r_pb[bankT].note_read(ev)
                r_rqcolp.note_write_partial(ev)
        else:
            thi = P.emit("vector",
                         lambda e: e.tensor_copy(rqkb_hi[:, :], rqrow[:, :]),
                         rd(r_rq) + wr(r_rqkh))
            r_rq.note_read(thi)
            r_rqkh.note_write(thi)
            tlo = P.emit("vector",
                         lambda e: e.tensor_tensor(rqrow[:, :], rqrow[:, :],
                                                   rqkb_hi[:, :], ALU.subtract),
                         rd(r_rq, r_rqkh))
            r_rq.note_write_partial(tlo)
            tlo2 = P.emit("vector",
                          lambda e: e.tensor_copy(rqkb_lo[:, :], rqrow[:, :]),
                          rd(r_rq) + wr(r_rqkl))
            r_rq.note_read(tlo2)
            r_rqkl.note_write(tlo2)
        cc0 = col0 + cch * 512
        tcols0 = h * HT + cch * 512
        uticks = {}

        def stageA(m):
            sl = m % 2
            uoff = sl * 512
            if which == 1:
                tb = P.emit("tensor",
                            lambda e, m=m:
                            e.matmul(pb[6][:, :], csts["hbm"][0:16, m * 128:(m + 1) * 128],
                                     rqkb_hi[:, :], start=True, stop=False),
                            rd(r_rqkh) + wr(r_pb[6]) + CST)
                r_rqkh.note_read(tb)
                r_pb[6].note_write(tb)
                tb2 = P.emit("tensor",
                             lambda e, m=m:
                             e.matmul(pb[6][:, :], csts["hbm"][0:16, m * 128:(m + 1) * 128],
                                      rqkb_lo[:, :], start=False, stop=True),
                             rd(r_rqkl) + CST)
                r_rqkl.note_read(tb2)
                r_pb[6].note_write_partial(tb2)
                th = P.emit("vector",
                            lambda e, m=m, buf=buf, sccol=sccol, i=i, sl=sl:
                            e.scalar_tensor_tensor(sD[sl][:, 0:512], buf[m][:, cc0:cc0 + 512],
                                                   sccol[:, i:i + 1], pb[6][:, :],
                                                   ALU.mult, ALU.mult),
                            rd(rbuf[m], r_pb[6]) + wr(r_sD[sl]) + CST)
                rbuf[m].note_read(th)
                r_pb[6].note_read(th)
                r_sD[sl].note_write(th)
                src_half = slice(0, 512)
            else:
                th = P.emit("vector",
                            lambda e, m=m, buf=buf, sccol=sccol, i=i, sl=sl:
                            e.tensor_scalar(sD[sl][:, 0:512], buf[m][:, cc0:cc0 + 512],
                                            sccol[:, i:i + 1], None, ALU.mult),
                            rd(rbuf[m]) + wr(r_sD[sl]) + CST)
                rbuf[m].note_read(th)
                r_sD[sl].note_write(th)
                th2 = P.emit("vector",
                             lambda e, sl=sl:
                             e.tensor_tensor(sD[sl][:, 512:1024], sD[sl][:, 0:512],
                                             csts["scl2"][:, tcols0:tcols0 + 512], ALU.mult),
                             rd(r_sD[sl]) + CST)
                r_sD[sl].note_write_partial(th2)
                src_half = slice(512, 1024)
            tu = P.emit("vector",
                        lambda e, sl=sl, src_half=src_half, uoff=uoff:
                        e.tensor_tensor(sD[2][:, uoff:uoff + 512], sD[sl][:, src_half],
                                        csts["ssk"][:, tcols0:tcols0 + 512], ALU.mult),
                        rd(r_sD[sl]) + wr(r_sD[2]) + CST)
            r_sD[sl].note_read(tu)
            r_sD[2].note_write_partial(tu)
            uticks[m] = (tu, src_half)

        def stageB(m):
            sl = m % 2
            uoff = sl * 512
            tu, src_half = uticks[m]
            tw = P.emit("tensor",
                        lambda e, uoff=uoff:
                        e.matmul(pb[3][:, :], csts["pp"][:, :], sD[2][:, uoff:uoff + 512],
                                 start=True, stop=True),
                        [tu] + wr(r_pb[3]) + CST)
            r_sD[2].note_read(tw)
            r_pb[3].note_write(tw)
            ta = P.emit("vector",
                        lambda e, sl=sl, src_half=src_half, uoff=uoff:
                        e.tensor_tensor(sD[2][:, uoff:uoff + 512], sD[sl][:, src_half],
                                        csts["csk"][:, tcols0:tcols0 + 512], ALU.mult),
                        rd(r_sD[sl]) + [tw] + CST)
            r_sD[sl].note_read(ta)
            r_sD[2].note_write_partial(ta)
            tf = P.emit("vector",
                        lambda e, m=m, buf=buf, uoff=uoff:
                        e.tensor_tensor(buf[m][:, cc0:cc0 + 512], sD[2][:, uoff:uoff + 512],
                                        pb[3][:, :], ALU.add),
                        rd(r_sD[2], r_pb[3]) + wr(rbuf[m]))
            r_sD[2].note_read(tf)
            r_pb[3].note_read(tf)
            rbuf[m].note_write_partial(tf)

        for m in range(NDT):
            stageA(m)
            if m > 0:
                stageB(m - 1)
        stageB(NDT - 1)

    def attention(i, h, ws):
        its = [(w, hd) for w in ws for hd in range(NH)]

        def stage_scores(idx):
            w, hd = its[idx]
            m = hd // 2
            p0 = (hd % 2) * 64
            wloc = w % 4
            par = idx % 2
            for qc in range(2):
                bank = 2 * par + qc
                qcol = wloc * 256 + qc * 128
                kcol = wloc * 256 + qc * 128
                maskid = (1 + qc) if w == 0 else 0
                nn = 384 if (wloc == 3 and qc == 1) else 512
                wl = rd(r_qb[m], r_kb[m]) + wr(r_pb[bank]) + CST
                mm = P.emit("tensor",
                            lambda e, m=m, p0=p0, qcol=qcol, kcol=kcol, bank=bank, nn=nn:
                            e.matmul(pb[bank][:, 0:nn], qb[m][p0:p0 + 64, qcol:qcol + 128],
                                     kb[m][p0:p0 + 64, kcol:kcol + nn],
                                     start=True, stop=False),
                            wl)
                r_qb[m].note_read(mm)
                r_kb[m].note_read(mm)
                r_pb[bank].note_write(mm)
                mk = P.emit("tensor",
                            lambda e, bank=bank, maskid=maskid, nn=nn:
                            e.matmul(pb[bank][:, 0:nn], csts["identb"][:, :],
                                     mask_ap(maskid, nn), start=False, stop=True),
                            CST)
                r_pb[bank].note_write_partial(mk)

        def stage_softmax(idx):
            par = idx % 2
            w, hd = its[idx]
            wloc = w % 4
            tch0 = wloc * 2
            rqc0 = (tch0 // 4) * 64 + (tch0 % 4) * 16 + hd
            rqview = rqcolp[:, :].rearrange("p (t g) -> p t g", g=16)
            sview = psS[par][:, :].rearrange("p (b j) -> p b j", b=2)[:, :, 0:384]
            tmx = P.emit("vector",
                         lambda e, par=par, sview=sview:
                         e.reduce_max(negmax2[:, 0:2], sview, axis=AX.X, negate=True),
                         rd(r_pb[2 * par], r_pb[2 * par + 1]) + wr(r_negmax2))
            r_pb[2 * par].note_read(tmx)
            r_pb[2 * par + 1].note_read(tmx)
            r_negmax2.note_write(tmx)
            tbm = P.emit("vector",
                         lambda e, tch0=tch0, hd=hd, rqview=rqview:
                         e.tensor_tensor(negmax2[:, 0:2], negmax2[:, 0:2],
                                         rqview[:, tch0:tch0 + 2, hd],
                                         ALU.mult),
                         rd(r_negmax2, r_rqcolp))
            r_rqcolp.note_read(tbm)
            r_negmax2.note_write(tbm)
            for qc in range(2):
                bank = 2 * par + qc
                rpr = r_probs4[qc][par]
                tch = wloc * 2 + qc
                rqc = (tch // 4) * 64 + (tch % 4) * 16 + hd
                pcol = (par * 2 + qc) * 512
                tex = P.emit("scalar",
                             lambda e, bank=bank, qc=qc, rqc=rqc, pcol=pcol:
                             e.activation(probsB[:, pcol:pcol + 384], pb[bank][:, 0:384],
                                          AF.Exp, bias=negmax2[:, qc:qc + 1],
                                          scale=rqcolp[:, rqc:rqc + 1],
                                          accum_out=zcol2[:, qc:qc + 1]),
                             [tbm] + wr(rpr, r_zcol2) + rd(r_rqcolp))
                r_pb[bank].note_read(tex)
                r_negmax2.note_read(tex)
                rpr.note_write(tex)
                r_zcol2.note_write_partial(tex)
            trz = P.emit("vector",
                         lambda e: e.reciprocal(zcol2[:, 0:2], zcol2[:, 0:2]),
                         rd(r_zcol2))
            r_zcol2.note_write(trz)
            for qc in range(2):
                rpr = r_probs4[qc][par]
                pcol = (par * 2 + qc) * 512
                tsc = P.emit("vector",
                             lambda e, qc=qc, pcol=pcol:
                             e.tensor_scalar(probsB[:, pcol:pcol + 384],
                                             probsB[:, pcol:pcol + 384],
                                             zcol2[:, qc:qc + 1], None, ALU.mult),
                             rd(rpr, r_zcol2))
                r_zcol2.note_read(tsc)
                rpr.note_write(tsc)

        pT_slots = [(probsT[:, :], r_probsT), (A[3][:, 0:1024], r_A[3])]

        PT_COLS = {(0, 0): 0, (0, 1): 128, (0, 2): 384,
                   (1, 0): 256, (1, 1): 512, (1, 2): 640}

        def stage_transp(idx):
            par = idx % 2
            poff = par * 512
            for qc in range(2):
                rpr = r_probs4[qc][par]
                for c in range(3):
                    col = PT_COLS[(qc, c)]
                    hh = col // 512
                    wl = rd(rpr) + CST
                    if qc == 0 and c == 0:
                        wl += wr(r_pTh[0]) + wr(r_pTh[1])
                    pcol = (par * 2 + qc) * 512
                    tt = P.emit("tensor",
                                lambda e, qc=qc, c=c, col=col, pcol=pcol:
                                e.transpose(pT[:, col:col + 128],
                                            probsB[:, pcol + c * 128: pcol + (c + 1) * 128],
                                            csts["identb"][:, :]),
                                wl)
                    rpr.note_read(tt)
                    r_pTh[hh].note_write_partial(tt)
            pts, ptres = pT_slots[idx % 2]
            tev = P.emit("scalar",
                         lambda e, pts=pts: e.copy(pts[:, 0:768], pT[:, 0:768]),
                         rd(r_pTh[0], r_pTh[1]) + wr(ptres))
            r_pTh[0].note_read(tev)
            r_pTh[1].note_read(tev)
            ptres.note_write(tev)

        def stage_av(idx):
            w, hd = its[idx]
            m = hd // 2
            p0 = (hd % 2) * 64
            wloc = w % 4
            pts, ptres = pT_slots[idx % 2]
            avbank = 4 + (hd // 2) % 2
            first = True
            for jt in (1, 0, 2, 3):
                tau = 2 * w - 2 + jt
                if tau < 0:
                    continue
                vslot = tau % 10
                if jt == 0:
                    oc0, oc1, pc0, pcn = 0, 128, 0, 128
                elif jt == 1:
                    oc0, oc1, pc0, pcn = 0, 256, 128, 256
                elif jt == 2:
                    oc0, oc1, pc0, pcn = 0, 256, 384, 256
                else:
                    oc0, oc1, pc0, pcn = 128, 256, 640, 128
                wl = rd(r_vb[vslot]) + rd(ptres)
                if first and p0 == 0:
                    wl += wr(r_pb[avbank])
                mm = P.emit("tensor",
                            lambda e, vslot=vslot, avbank=avbank, p0=p0, hd=hd,
                                   pts=pts, s=first, p=(jt == 3),
                                   oc0=oc0, oc1=oc1, pc0=pc0, pcn=pcn:
                            e.matmul(pb[avbank][p0:p0 + 64, oc0:oc1],
                                     vb[vslot][:, hd * 64:(hd + 1) * 64],
                                     pts[:, pc0:pc0 + pcn],
                                     start=s, stop=p),
                            wl)
                first = False
                r_vb[vslot].note_read(mm)
                ptres.note_read(mm)
                r_pb[avbank].note_write_partial(mm)
            if hd % 2 == 1:
                ev = P.emit("vector",
                            lambda e, m=m, wloc=wloc, avbank=avbank:
                            e.tensor_copy(qb[m][:, wloc * 256:(wloc + 1) * 256],
                                          pb[avbank][:, 0:256]),
                            rd(r_pb[avbank]) + wr(r_qb[m]))
                r_pb[avbank].note_read(ev)
                r_qb[m].note_write_partial(ev)

        n = len(its)
        for idx in range(n):
            stage_scores(idx)
            if idx > 1:
                stage_transp(idx - 2)
            if idx > 2:
                stage_av(idx - 3)
            stage_softmax(idx)
        stage_transp(n - 2)
        stage_av(n - 3)
        stage_transp(n - 1)
        stage_av(n - 2)
        stage_av(n - 1)

    def wo_phase(i, h):
        wts = {}
        def ld(m):
            if m < NDT and m not in wts:
                wts[m] = load_rot(w_wo[i, :, m * 128:(m + 1) * 128]
                                  .rearrange("(kc p) m -> p kc m", p=128), m % 3,
                                  three_d=True)
        ld(0); ld(1)
        for m in range(NDT):
            slot = m % 3
            ld(m + 1)
            wt = wts[m]
            for k in range(NDT):
                for n in range(2):
                    bank = (m * 2 + n) % 4
                    wl = [wt] + rd(r_qb[k]) + CST
                    if k == 0:
                        wl += wr(r_pb[bank])
                    mm = P.emit("tensor",
                                lambda e, slot=slot, k=k, n=n, bank=bank,
                                       s=(k == 0), p=(k == NDT - 1):
                                e.matmul(pb[bank][:, :], A[slot][:, k * 128:(k + 1) * 128],
                                         qb[k][:, n * 512:(n + 1) * 512],
                                         start=s, stop=p),
                                wl)
                    r_A[slot].note_read(mm)
                    r_qb[k].note_read(mm)
                    r_pb[bank].note_write_partial(mm)
            for n in range(2):
                bank = (m * 2 + n) % 4
                ev = P.emit("vector",
                            lambda e, m=m, n=n, h=h, bank=bank:
                            e.tensor_tensor(xres[m][:, h * HT + n * 512: h * HT + n * 512 + 512],
                                            xres[m][:, h * HT + n * 512: h * HT + n * 512 + 512],
                                            pb[bank][:, :], ALU.add),
                            rd(r_pb[bank]) + wr(r_xres[m]))
                r_pb[bank].note_read(ev)
                r_xres[m].note_write_partial(ev)

    def ffn_phase(i, h, next_role=None):
        offs = SHIFT_OFFS
        ag_tiles = []
        for p in range(NCH):
            if p < 8:
                ag_tiles.append((qb[p], r_qb[p], 0))
            elif p < 16:
                ag_tiles.append((kb[p - 8], r_kb[p - 8], 256))
            else:
                ag_tiles.append((vb[p - 16], r_vb[p - 16], 0))
        w1ld = {}
        def ldw1(p):
            if p < NCH and p not in w1ld:
                w1ld[p] = (load_w1(i, p, False, (2 * p) % 5),
                           load_w1(i, p, True, (2 * p + 1) % 5))
        ldw1(0); ldw1(1)
        for p in range(NCH):
            off, sz = FFN_CHUNKS[p]
            sa = (2 * p) % 5
            sg = (2 * p + 1) % 5
            ta, tg = w1ld[p]
            for n in range(2):
                banka, bankg = n, 2 + n
                for k in range(NDT):
                    wl = [ta] + rd(r_role[k]) + CST
                    if k == 0:
                        wl += wr(r_pb[banka])
                    mm = P.emit("tensor",
                                lambda e, sa=sa, k=k, n=n, banka=banka, sz=sz, o=offs[k],
                                       s=(k == 0), q=(k == NDT - 1):
                                e.matmul(pb[banka][0:sz, :], A[sa][:, k * 128:k * 128 + sz],
                                         role[k][:, o + n * 512: o + n * 512 + 512],
                                         start=s, stop=q),
                                wl)
                    r_A[sa].note_read(mm)
                    r_role[k].note_read(mm)
                    r_pb[banka].note_write_partial(mm)
                for k in range(NDT):
                    wl = [tg] + rd(r_role[k]) + CST
                    if k == 0:
                        wl += wr(r_pb[bankg])
                    mm = P.emit("tensor",
                                lambda e, sg=sg, k=k, n=n, bankg=bankg, sz=sz, o=offs[k],
                                       s=(k == 0), q=(k == NDT - 1):
                                e.matmul(pb[bankg][0:sz, :], A[sg][:, k * 128:k * 128 + sz],
                                         role[k][:, o + n * 512: o + n * 512 + 512],
                                         start=s, stop=q),
                                wl)
                    r_A[sg].note_read(mm)
                    r_role[k].note_read(mm)
                    r_pb[bankg].note_write_partial(mm)
                bc = i * NCH + p
                gslot = n
                tge = P.emit("scalar",
                             lambda e, bankg=bankg, gslot=gslot, sz=sz, bc=bc:
                             e.activation(probsB[0:sz, gslot * 512:gslot * 512 + 512], pb[bankg][0:sz, :], AF.Gelu,
                                          bias=csts["b1gcol"][0:sz, bc:bc + 1]),
                             rd(r_pb[bankg]) + wr(r_probs4[gslot][0]) + CST)
                r_pb[bankg].note_read(tge)
                r_probs4[gslot][0].note_write(tge)
                agbuf, agres, agcol = ag_tiles[p]
                tag = P.emit("vector",
                             lambda e, banka=banka, gslot=gslot, sz=sz, bc=bc,
                                    agbuf=agbuf, agcol=agcol, n=n:
                             e.scalar_tensor_tensor(
                                 agbuf[0:sz, agcol + n * 512: agcol + n * 512 + 512],
                                 pb[banka][0:sz, :], csts["b1acol"][0:sz, bc:bc + 1],
                                 probsB[0:sz, gslot * 512:gslot * 512 + 512], ALU.add, ALU.mult),
                             rd(r_pb[banka], r_probs4[gslot][0]) + wr(agres) + CST)
                r_pb[banka].note_read(tag)
                r_probs4[gslot][0].note_read(tag)
                agres.note_write_partial(tag)
            ldw1(p + 2)
        w2ld = {}
        def ldw2(mo):
            if mo < NDT and mo not in w2ld:
                w2ld[mo] = load_w2(i, mo, mo % 2)
        ldw2(0); ldw2(1)
        for mo in range(NDT):
            j = mo % 2
            ldw2(mo + 1)
            wts = w2ld[mo]
            for n in range(2):
                bank = 4 + n
                for p in range(NCH):
                    off, sz = FFN_CHUNKS[p]
                    agbuf, agres, agcol = ag_tiles[p]
                    wl = wts + rd(agres)
                    if p == 0:
                        wl += wr(r_pb[bank])
                    mm = P.emit("tensor",
                                lambda e, j=j, p=p, n=n, bank=bank, sz=sz,
                                       agbuf=agbuf, agcol=agcol,
                                       s=(p == 0), q=(p == NCH - 1):
                                e.matmul(pb[bank][:, :], w2v[j][0:sz, p * 128:(p + 1) * 128],
                                         agbuf[0:sz, agcol + n * 512: agcol + n * 512 + 512],
                                         start=s, stop=q),
                                wl)
                    for s2 in W2_SLOTMAP[j]:
                        r_A[s2].note_read(mm)
                    agres.note_read(mm)
                    r_pb[bank].note_write_partial(mm)
                gc = i * 8 + mo
                ev = P.emit("vector",
                            lambda e, mo=mo, n=n, h=h, bank=bank, gc=gc:
                            e.scalar_tensor_tensor(
                                xres[mo][:, h * HT + n * 512: h * HT + n * 512 + 512],
                                pb[bank][:, :], csts["b2col"][:, gc:gc + 1],
                                xres[mo][:, h * HT + n * 512: h * HT + n * 512 + 512],
                                ALU.add, ALU.add),
                            rd(r_pb[bank]) + wr(r_xres[mo]) + CST)
                r_pb[bank].note_read(ev)
                r_xres[mo].note_write_partial(ev)
            if next_role is not None:
                role_copy_tile(next_role[0], next_role[1], mo)

    def transpose_weight_to_wv(src_dram):
        for m in range(NDT):
            slot = m % 3
            tk = load_rot(src_dram[m * 128:(m + 1) * 128, :], slot)
            for d in range(NDT):
                wl = [tk] + CST
                if d == 0:
                    wl += wr(r_pTh[0]) + wr(r_pTh[1])
                tt = P.emit("tensor",
                            lambda e, slot=slot, d=d:
                            e.transpose(pT[:, d * 128:(d + 1) * 128],
                                        A[slot][:, d * 128:(d + 1) * 128],
                                        csts["identb"][:, :]),
                            wl)
                r_A[slot].note_read(tt)
                r_pTh[d // 4].note_write_partial(tt)
            for d in range(NDT):
                ev = P.emit("scalar",
                            lambda e, d=d, m=m:
                            e.copy(A[3 + d][:, m * 128:(m + 1) * 128],
                                   pT[:, d * 128:(d + 1) * 128]),
                            rd(r_pTh[0], r_pTh[1]) + wr(r_A[3 + d]))
                r_pTh[0].note_read(ev)
                r_pTh[1].note_read(ev)
                r_A[3 + d].note_write_partial(ev)

    def outer_attention():
        for h in range(2):
            role_copy(h, shifted=False)
            banks = ln_stats([0] * NDT)
            ln_apply_role([0] * NDT, banks, layer=None)
            transpose_weight_to_wv(w_aWv)
            for m in range(NDT):
                for n in range(2):
                    bank = (m * 2 + n) % 4
                    for k in range(NDT):
                        wl = rd(r_A[3 + k], r_role[k]) + CST
                        if k == 0:
                            wl += wr(r_pb[bank])
                        mm = P.emit("tensor",
                                    lambda e, k=k, m=m, n=n, bank=bank,
                                           s=(k == 0), p=(k == NDT - 1):
                                    e.matmul(pb[bank][:, :], A[3 + k][:, m * 128:(m + 1) * 128],
                                             role[k][:, n * 512:(n + 1) * 512],
                                             start=s, stop=p),
                                    wl)
                        r_A[3 + k].note_read(mm)
                        r_role[k].note_read(mm)
                        r_pb[bank].note_write_partial(mm)
                    ev = P.emit("scalar",
                                lambda e, m=m, n=n, bank=bank:
                                e.copy(qb[m][:, n * 512:(n + 1) * 512], pb[bank][:, :]),
                                rd(r_pb[bank]) + wr(r_qb[m]))
                    r_pb[bank].note_read(ev)
                    r_qb[m].note_write_partial(ev)
            for n in range(2):
                tst = P.emit("vector",
                             lambda e, h=h, n=n:
                             e.tensor_copy(rowscr[0:1, n * 512:(n + 1) * 512],
                                           rows2[32 + 32 * h:33 + 32 * h,
                                                 n * 512:(n + 1) * 512]),
                             CST + wr(r_rowscr))
                r_rowscr.note_write_partial(tst)
            for d in range(NDT):
                for n in range(2):
                    bank = 4 + n
                    tp = P.emit("tensor",
                                lambda e, d=d, n=n, h=h, bank=bank:
                                e.matmul(pb[bank][:, :],
                                         freqsrow[0:1, d * 128:(d + 1) * 128],
                                         rowscr[0:1, n * 512:(n + 1) * 512],
                                         start=True, stop=True),
                                CST + rd(r_rowscr) + wr(r_pb[bank]))
                    r_pb[bank].note_write(tp)
                    ioA = iostage[0][:, 0:512]
                    ioB = iostage[0][:, 512:1024]
                    ioBint = ioB.bitcast(mybir.dt.int32)
                    for off, dst, rdst in ((0.0, sD[0], r_sD[0]),
                                           (0.5 * PI, sD[1], r_sD[1])):
                        t1 = P.emit("vector",
                                    lambda e, bank=bank, off=off:
                                    e.tensor_scalar(ioA, pb[bank][:, :], off,
                                                    1.0 / (2 * PI), ALU.add, ALU.mult),
                                    rd(r_pb[bank]) + [tp] + wr(r_io[0]))
                        r_pb[bank].note_read(t1)
                        r_io[0].note_write(t1)
                        t2 = P.emit("vector",
                                    lambda e: e.tensor_copy(ioBint, ioA),
                                    rd(r_io[0]))
                        r_io[0].note_write_partial(t2)
                        t3 = P.emit("vector",
                                    lambda e: e.tensor_tensor(ioA, ioA, ioBint,
                                                              ALU.subtract),
                                    rd(r_io[0]))
                        r_io[0].note_write_partial(t3)
                        t4 = P.emit("vector",
                                    lambda e: e.tensor_scalar(ioB, ioA, 0.5, None,
                                                              ALU.is_ge),
                                    rd(r_io[0]))
                        r_io[0].note_write_partial(t4)
                        t5 = P.emit("vector",
                                    lambda e: e.tensor_tensor(ioA, ioA, ioB,
                                                              ALU.subtract),
                                    rd(r_io[0]))
                        r_io[0].note_write_partial(t5)
                        ts = P.emit("scalar",
                                    lambda e, n=n, dst=dst:
                                    e.activation(dst[:, n * 512:(n + 1) * 512], ioA,
                                                 AF.Sin, scale=2 * PI),
                                    rd(r_io[0]) + wr(rdst))
                        r_io[0].note_read(ts)
                        rdst.note_write_partial(ts)
                tvc = P.emit("vector",
                             lambda e, d=d:
                             e.tensor_tensor(sD[2][:, :], qb[d][:, :], sD[1][:, :], ALU.mult),
                             rd(r_qb[d], r_sD[1]) + wr(r_sD[2]))
                r_qb[d].note_read(tvc)
                r_sD[1].note_read(tvc)
                r_sD[2].note_write(tvc)
                tsc = P.emit("vector",
                             lambda e, d=d:
                             e.tensor_tensor_scan(sD[3][:, :], sD[2][:, :], sD[2][:, :],
                                                  carry[:, d:d + 1], ALU.add, ALU.bypass),
                             rd(r_sD[2], r_carry) + wr(r_sD[3]))
                r_sD[2].note_read(tsc)
                r_carry.note_read(tsc)
                r_sD[3].note_write(tsc)
                tcy = P.emit("vector",
                             lambda e, d=d: e.tensor_copy(carry[:, d:d + 1], sD[3][:, 1023:1024]),
                             rd(r_sD[3]) + wr(r_carry))
                r_sD[3].note_read(tcy)
                r_carry.note_write_partial(tcy)
                tr1 = P.emit("vector",
                             lambda e: e.tensor_tensor(sD[1][:, :], sD[1][:, :], sD[3][:, :],
                                                       ALU.mult),
                             rd(r_sD[1], r_sD[3]))
                r_sD[3].note_read(tr1)
                r_sD[1].note_write(tr1)
                tvs = P.emit("vector",
                             lambda e, d=d:
                             e.tensor_tensor(sD[2][:, :], qb[d][:, :], sD[0][:, :], ALU.mult),
                             rd(r_qb[d], r_sD[0]) + wr(r_sD[2]))
                r_qb[d].note_read(tvs)
                r_sD[0].note_read(tvs)
                r_sD[2].note_write(tvs)
                tsb = P.emit("vector",
                             lambda e, d=d:
                             e.tensor_tensor_scan(sD[3][:, :], sD[2][:, :], sD[2][:, :],
                                                  carry[:, 8 + d:9 + d], ALU.add, ALU.bypass),
                             rd(r_sD[2], r_carry) + wr(r_sD[3]))
                r_sD[2].note_read(tsb)
                r_carry.note_read(tsb)
                r_sD[3].note_write(tsb)
                tcy2 = P.emit("vector",
                              lambda e, d=d:
                              e.tensor_copy(carry[:, 8 + d:9 + d], sD[3][:, 1023:1024]),
                              rd(r_sD[3]) + wr(r_carry))
                r_sD[3].note_read(tcy2)
                r_carry.note_write_partial(tcy2)
                tr2 = P.emit("vector",
                             lambda e: e.tensor_tensor(sD[0][:, :], sD[0][:, :], sD[3][:, :],
                                                       ALU.mult),
                             rd(r_sD[0], r_sD[3]))
                r_sD[3].note_read(tr2)
                r_sD[0].note_write(tr2)
                tr3 = P.emit("vector",
                             lambda e, d=d:
                             e.tensor_tensor(kb[d][:, 0:1024], sD[1][:, :], sD[0][:, :],
                                             ALU.add),
                             rd(r_sD[1], r_sD[0]) + wr(r_kb[d]))
                r_sD[1].note_read(tr3)
                r_sD[0].note_read(tr3)
                r_kb[d].note_write(tr3)
            transpose_weight_to_wv(w_aWo)
            for m in range(NDT):
                for n in range(2):
                    bank = (m * 2 + n) % 4
                    for k in range(NDT):
                        wl = rd(r_A[3 + k], r_kb[k]) + CST
                        if k == 0:
                            wl += wr(r_pb[bank])
                        mm = P.emit("tensor",
                                    lambda e, k=k, m=m, n=n, bank=bank,
                                           s=(k == 0), p=(k == NDT - 1):
                                    e.matmul(pb[bank][:, :], A[3 + k][:, m * 128:(m + 1) * 128],
                                             kb[k][:, n * 512:(n + 1) * 512],
                                             start=s, stop=p),
                                    wl)
                        r_A[3 + k].note_read(mm)
                        r_kb[k].note_read(mm)
                        r_pb[bank].note_write_partial(mm)
                    ev = P.emit("vector",
                                lambda e, m=m, n=n, h=h, bank=bank:
                                e.tensor_tensor(
                                    xres[m][:, h * HT + n * 512: h * HT + n * 512 + 512],
                                    xres[m][:, h * HT + n * 512: h * HT + n * 512 + 512],
                                    pb[bank][:, :], ALU.add),
                                rd(r_pb[bank]) + wr(r_xres[m]))
                    r_pb[bank].note_read(ev)
                    r_xres[m].note_write_partial(ev)

    def save_x2_ln2():
        ticks = []
        for d in range(NDT):
            tk = P.dma("sync", x2_dram[d], xres[d][:, :], ds_x2, rd(r_xres[d]))
            r_xres[d].note_read(tk)
            r_x2d.note_write_partial(tk)
            ticks.append(tk)
        mx = max(t.dma_value for t in ticks)
        for t in ticks:
            t.dma_value = mx
        for h in range(2):
            role_copy(h, shifted=False)
            banks = ln_stats([0] * NDT)
            ln_apply_xres(h, banks)

    def finale():
        ticks = []
        for d in range(NDT):
            tk = P.dma("gpsimd", xres[d][:, :], x2_dram[d], ds_x2r,
                       rd(r_x2d) + wr(r_xres[d]), accum=ALU.add)
            r_xres[d].note_write(tk)
            ticks.append(tk)
        mx = max(t.dma_value for t in ticks)
        for t in ticks:
            t.dma_value = mx
        for tau in range(16):
            st = tau % 2
            rs = [r_A[2 * st], r_A[2 * st + 1]]
            evs = []
            for d in range(NDT):
                bank = (tau * NDT + d) % 4
                mm = P.emit("tensor",
                            lambda e, d=d, tau=tau, bank=bank:
                            e.transpose(pb[bank][:, 0:128],
                                        xres[d][:, tau * 128:(tau + 1) * 128],
                                        csts["identf"][:, :]),
                            rd(r_xres[d]) + wr(r_pb[bank]) + CST)
                r_xres[d].note_read(mm)
                r_pb[bank].note_write(mm)
                wl = rd(r_pb[bank]) + (wr(*rs) if d == 0 else [])
                ev = P.emit("scalar",
                            lambda e, d=d, st=st, bank=bank:
                            e.copy(stg[st][:, d * 128:(d + 1) * 128], pb[bank][:, 0:128]),
                            wl)
                r_pb[bank].note_read(ev)
                for r in rs:
                    r.note_write_partial(ev)
                evs.append(ev)
            tk = P.dma("sync", y_out[tau * 128:(tau + 1) * 128, :], stg[st],
                       ds_ys[st], evs)
            for r in rs:
                r.note_read(tk)

    load_input()
    outer_attention()
    tap("x2pre")
    save_x2_ln2()
    tap("ln2")
    for i in range(DEPTH):
        for h in range(2):
            if i == 0 and h == 0:
                role_copy(h, shifted=True)
            qkv_qk(i, h)
            qkv_v(i, h)
            qk_prep(i, h, 0, 0)
            qk_prep(i, h, 1, 0)
            qk_prep(i, h, 0, 1)
            qk_prep(i, h, 1, 1)
            attention(i, h, [h * 4, h * 4 + 1])
            attention(i, h, [h * 4 + 2, h * 4 + 3])
            wo_phase(i, h)
            if h == 0:
                for m in range(NDT):
                    tkh = P.emit("vector",
                                 lambda e, m=m:
                                 e.tensor_copy(kb[m][:, 0:256], kb[m][:, 1024:1280]),
                                 wr(r_kb[m]))
                    r_kb[m].note_write_partial(tkh)
            tap(f"attn{i}h{h}")
            role_copy(h, shifted=True)
            banks = ln_stats(SHIFT_OFFS)
            ln_apply_role(SHIFT_OFFS, banks, layer=i)
            if h == 0:
                nxt = (1, True)
            elif i < DEPTH - 1:
                nxt = (0, True)
            else:
                nxt = None
            ffn_phase(i, h, next_role=nxt)
            tap(f"layer{i}h{h}")
    finale()

    with nc.Block() as block:
        P.realize(block, eng_sems, dma_handles)

        @block.sync
        def _(sync):
            sync.wait_ge(dma_handles[ds_ys[0].key], 16 * ds_ys[0].count)
            sync.wait_ge(dma_handles[ds_ys[1].key], 16 * ds_ys[1].count)

    stack.close()
    return nc


def kernel(L0_attn_x, L0_attn_Wk, L0_attn_Wv, L0_attn_Wo, L0_attn_freqs,
           L0_ffn_Wqkv, L0_ffn_q_scale, L0_ffn_k_scale, L0_ffn_Wo,
           L0_ffn_ln_g, L0_ffn_ln_b, L0_ffn_W1, L0_ffn_b1, L0_ffn_W2,
           L0_ffn_b2):
    global LAST
    inp = dict(L0_ffn_ln_g=L0_ffn_ln_g, L0_ffn_ln_b=L0_ffn_ln_b,
               L0_ffn_b2=L0_ffn_b2, L0_ffn_q_scale=L0_ffn_q_scale,
               L0_ffn_k_scale=L0_ffn_k_scale, L0_ffn_b1=L0_ffn_b1,
               L0_attn_freqs=L0_attn_freqs)
    C = host_constants()
    V = pack_vectors(inp)
    x = np.ascontiguousarray(np.asarray(L0_attn_x, np.float32))

    nc = build_program()

    base = {
        "aWv": np.ascontiguousarray(np.asarray(L0_attn_Wv, np.float32)),
        "aWo": np.ascontiguousarray(np.asarray(L0_attn_Wo, np.float32)),
        "Wqkv": np.ascontiguousarray(np.asarray(L0_ffn_Wqkv, np.float32)),
        "Wo": np.ascontiguousarray(np.asarray(L0_ffn_Wo, np.float32)),
        "W1": np.ascontiguousarray(np.asarray(L0_ffn_W1, np.float32)),
        "W2": np.ascontiguousarray(np.asarray(L0_ffn_W2, np.float32)),
    }
    for nm, arr in C.items():
        base["c_" + nm] = np.ascontiguousarray(arr)
    for nm, arr in V.items():
        base["c_" + nm] = np.ascontiguousarray(arr)

    in_maps = []
    for c in range(NCORES):
        m = dict(base)
        m["x"] = np.ascontiguousarray(x[c])
        in_maps.append(m)

    res = run_bass_kernel_spmd(nc, in_maps, core_ids=list(range(NCORES)))
    LAST = res
    out = np.stack([np.asarray(r["y"], np.float32) for r in res.results], axis=0)
    return out
